# version 05 (final)
# baseline (speedup 1.0000x reference)
import numpy as np
from contextlib import ExitStack
import concourse.bass as bass
import concourse.mybir as mybir
from concourse.bass_utils import run_bass_kernel_spmd

F32 = mybir.dt.float32
BF16 = mybir.dt.bfloat16
AF = mybir.ActivationFunctionType
ALU = mybir.AluOpType
AX = mybir.AxisListType

D = 2048
S = 4096
T = 1024
NPASS = S // T
KD = D // 128
E = 8
I = 1408
NI = I // 128
IS = 2816
NIS = IS // 128
H = 16
SCALE = float(1.0 / np.sqrt(128.0))
EPS = 1e-5

ENGINES = ("tensor", "vector", "scalar", "gpsimd", "sync")


class Builder:

    def __init__(self):
        self.ops = []
        self.sem_keys = {"e_" + e for e in ENGINES}

    def op(self, eng, fn, reads=(), writes=(), dma_sem=None, tag="", deps=()):
        if dma_sem is not None:
            self.sem_keys.add(dma_sem)
        self.ops.append(dict(
            eng=eng, fn=fn, reads=tuple(reads), writes=tuple(writes),
            sem=("e_" + eng) if dma_sem is None else dma_sem,
            step=1 if dma_sem is None else 16,
            tag=tag, deps=set(deps), inc=False, cum=0, waits={},
        ))

    def barrier(self):
        last = {}
        for i, o in enumerate(self.ops):
            if o["fn"] is not None:
                last[o["sem"]] = i
        idxs = set(last.values())
        for e in ENGINES:
            self.op(e, None, tag="barrier", deps=idxs)

    def resolve(self):
        last_w = {}
        readers = {}
        for i, o in enumerate(self.ops):
            deps = set(o["deps"])
            for r in o["reads"]:
                j = last_w.get(r)
                if j is not None:
                    deps.add(j)
            for w in o["writes"]:
                j = last_w.get(w)
                if j is not None:
                    deps.add(j)
                deps |= readers.get(w, set())
            o["deps"] = set()
            for j in deps:
                if j == i:
                    continue
                p = self.ops[j]
                if p["eng"] == "tensor" and o["eng"] == "tensor" and p["fn"] is not None and o["fn"] is not None:
                    continue
                o["deps"].add(j)
            for r in o["reads"]:
                readers.setdefault(r, set()).add(i)
            for w in o["writes"]:
                last_w[w] = i
                readers[w] = set()
        for i, o in enumerate(self.ops):
            expand = set()
            for j in o["deps"]:
                if self.ops[j]["fn"] is None:
                    expand |= self.ops[j]["deps"]
            o["deps"] |= expand
            o["deps"] = {j for j in o["deps"] if self.ops[j]["fn"] is not None}
        for o in self.ops:
            if o["step"] == 16 and o["fn"] is not None:
                o["inc"] = True
        for o in self.ops:
            for j in o["deps"]:
                self.ops[j]["inc"] = True
        cums = {k: 0 for k in self.sem_keys}
        for o in self.ops:
            if o["inc"] and o["fn"] is not None:
                cums[o["sem"]] += o["step"]
                o["cum"] = cums[o["sem"]]
        floors = {e: {} for e in ENGINES}
        for o in self.ops:
            waits = {}
            for j in o["deps"]:
                p = self.ops[j]
                assert p["inc"] and p["cum"] > 0, (p["tag"], o["tag"])
                if p["sem"] == o["sem"] and p["eng"] == o["eng"] and p["sem"].startswith("e_"):
                    pass
                fl = floors[o["eng"]].get(p["sem"], 0)
                if p["cum"] > fl:
                    waits[p["sem"]] = max(waits.get(p["sem"], 0), p["cum"])
            for k, v in waits.items():
                floors[o["eng"]][k] = v
            o["waits"] = waits
        self.max_cums = cums

    def playback(self, nc, ctx, dram_final_waits):
        sems = {k: ctx.enter_context(nc.semaphore(k)) for k in sorted(self.sem_keys)}
        block = ctx.enter_context(nc.Block())
        streams = {e: [o for o in self.ops if o["eng"] == e] for e in ENGINES}

        def run_stream(proxy, eng):
            for o in streams[eng]:
                for k, v in sorted(o["waits"].items()):
                    proxy.wait_ge(sems[k], v)
                if o["fn"] is None:
                    continue
                ins = o["fn"](proxy)
                if o["inc"]:
                    assert ins is not None, o["tag"]
                    ins.then_inc(sems[o["sem"]], o["step"])
            if eng == "gpsimd":
                for k, v in dram_final_waits:
                    proxy.wait_ge(sems[k], v)

        @block.tensor
        def _(t):
            run_stream(t, "tensor")

        @block.vector
        def _(v):
            run_stream(v, "vector")

        @block.scalar
        def _(s):
            run_stream(s, "scalar")

        @block.gpsimd
        def _(g):
            run_stream(g, "gpsimd")

        @block.sync
        def _(sy):
            run_stream(sy, "sync")
        return sems



def _blk(w):
    Co, Ci = w.shape
    return np.ascontiguousarray(
        w.reshape(Co // 128, 128, Ci // 128, 128).transpose(3, 0, 2, 1).reshape(128, -1))


def _blk12(w1e, w2e):
    nic = w1e.shape[0] // 128
    a = w1e.reshape(nic, 128, KD, 128)
    b = w2e.reshape(nic, 128, KD, 128)
    s = np.stack([a, b], axis=1)
    return np.ascontiguousarray(s.transpose(4, 0, 1, 3, 2).reshape(128, -1))


def prep_inputs(inp):
    f = lambda a: np.asarray(a, dtype=np.float32)
    w1 = f(inp["L0_ffn_w1"]); w2 = f(inp["L0_ffn_w2"]); w3 = f(inp["L0_ffn_w3"])
    sw1 = f(inp["L0_ffn_sw1"]); sw2 = f(inp["L0_ffn_sw2"]); sw3 = f(inp["L0_ffn_sw3"])
    shared = {
        "w12": np.stack([_blk12(w1[e], w2[e]) for e in range(E)]),
        "w3b": np.stack([_blk(w3[e]) for e in range(E)]),
        "w12s": _blk12(sw1, sw2),
        "w3s": _blk(sw3),
        "gwt": np.ascontiguousarray(
            f(inp["L0_ffn_gate_w"]).T.reshape(KD, 128, E).transpose(1, 0, 2).reshape(128, KD * E)),
        "wqb": _blk(f(inp["L0_attn_wq"])),
        "wkb": _blk(f(inp["L0_attn_wk"])),
        "wvb": _blk(f(inp["L0_attn_wv"])),
        "wob": _blk(f(inp["L0_attn_wo"])),
        "bqm": np.ascontiguousarray(f(inp["L0_attn_bq"]).reshape(KD, 128).T),
        "bkm": np.ascontiguousarray(f(inp["L0_attn_bk"]).reshape(KD, 128).T),
        "bom": np.ascontiguousarray(f(inp["L0_attn_bo"]).reshape(KD, 128).T),
        "bvr": np.ascontiguousarray(f(inp["L0_attn_bv"]).reshape(1, D)),
        "ones128": np.ones((128, 1), np.float32),
        "ones1": np.ones((1, 128), np.float32),
        "ident": np.eye(128, dtype=np.float32),
    }
    x = f(inp["L0_attn_hidden_states"])
    return [dict(shared, x_fm=np.ascontiguousarray(x[b].T)) for b in range(8)]



class MK:
    def __init__(self, cfg):
        self.cfg = cfg
        self.nc = bass.Bass("TRN2", target_bir_lowering=False)
        self.B = Builder()
        self.ctx = ExitStack()
        self.SLAB_BYTES = 205 * 1024
        self._slab_off = 0

    def dram(self, name, shape, dtype, kind=None):
        t = (self.nc.dram_tensor(name, shape, dtype, kind=kind) if kind
             else self.nc.dram_tensor(name, shape, dtype))
        return t.ap()

    def carve(self, name, shape, dtype):
        nbytes = int(np.prod(shape[1:])) * mybir.dt.size(dtype)
        off = (self._slab_off + 31) // 32 * 32
        self._slab_off = off + nbytes
        assert self._slab_off <= self.SLAB_BYTES, (name, self._slab_off)
        ap = self.slab[:, off:off + nbytes].bitcast(dtype)
        return ap[0:shape[0], 0:shape[1]]

    def reset_slab(self):
        self._slab_off = 0

    def mm_group(self, plan, reads, writes, tag=""):
        def fn(t, plan=plan):
            ins = None
            for (o, l, r, st, sp) in plan:
                ins = t.matmul(o, l, r, start=st, stop=sp, skip_group_check=True)
            return ins
        self.B.op("tensor", fn, reads=reads, writes=writes, tag=tag)

    def dma(self, eng, dst, src, sem, reads, writes, tag=""):
        self.B.op(eng, lambda e, dst=dst, src=src: e.dma_start(dst, src),
                  reads=reads, writes=writes, dma_sem=sem, tag=tag)

    def build(self):
        nc, B, cfg = self.nc, self.B, self.cfg
        self.x_fm = self.dram("x_fm", [D, S], F32, kind="ExternalInput")
        self.out_fm = self.dram("out_fm", [D, S], F32, kind="ExternalOutput")
        self.w12 = self.dram("w12", [E, 128, NI * 2 * D], F32, kind="ExternalInput")
        self.w3b = self.dram("w3b", [E, 128, KD * I], F32, kind="ExternalInput")
        self.w12s = self.dram("w12s", [128, NIS * 2 * D], F32, kind="ExternalInput")
        self.w3s = self.dram("w3s", [128, KD * IS], F32, kind="ExternalInput")
        self.gwt = self.dram("gwt", [128, KD * E], F32, kind="ExternalInput")
        self.wqb = self.dram("wqb", [128, H * D], F32, kind="ExternalInput")
        self.wkb = self.dram("wkb", [128, H * D], F32, kind="ExternalInput")
        self.wvb = self.dram("wvb", [128, H * D], F32, kind="ExternalInput")
        self.wob = self.dram("wob", [128, KD * D], F32, kind="ExternalInput")
        self.bqm = self.dram("bqm", [128, KD], F32, kind="ExternalInput")
        self.bkm = self.dram("bkm", [128, KD], F32, kind="ExternalInput")
        self.bom = self.dram("bom", [128, KD], F32, kind="ExternalInput")
        self.bvr = self.dram("bvr", [1, D], F32, kind="ExternalInput")
        self.ones128_d = self.dram("ones128", [128, 1], F32, kind="ExternalInput")
        self.ones1_d = self.dram("ones1", [1, 128], F32, kind="ExternalInput")
        self.ident_d = self.dram("ident", [128, 128], F32, kind="ExternalInput")
        self.w12c = self.dram("w12c", [E, 128, NI * 2 * D], BF16)
        self.w3c = self.dram("w3c", [E, 128, KD * I], BF16)
        self.w12sc = self.dram("w12sc", [128, NIS * 2 * D], BF16)
        self.w3sc = self.dram("w3sc", [128, KD * IS], BF16)
        self.xhat1 = self.dram("xhat1", [D, S], F32)
        self.n1 = self.dram("n1", [D, S], BF16)
        self.xhat2 = self.dram("xhat2", [D, S], F32)
        self.n2 = self.dram("n2", [D, S], BF16)
        self.att_s = self.dram("att_s", [D, S], BF16)
        self.cmb_scr = self.dram("cmb_scr", [64, 128], F32)

        sb = lambda n_, s, d=F32: self.ctx.enter_context(nc.sbuf_tensor(n_, s, d))
        self.slab = sb("slab", [128, self.SLAB_BYTES], mybir.dt.uint8)
        self.ones128_bf = sb("ones128_bf", [128, 1], BF16)
        self.ones128_f = sb("ones128_f", [128, 1], F32)
        self.ones1_f = sb("ones1_f", [1, 128], F32)
        self.ident = sb("identsb", [128, 128], F32)
        self.gwt_bf = sb("gwt_bf", [128, KD * E], BF16)
        self.bq_sb = sb("bq_sb", [128, KD], F32)
        self.bk_sb = sb("bk_sb", [128, KD], F32)
        self.bo_sb = sb("bo_sb", [128, KD], F32)
        self.pw = [self.ctx.enter_context(nc.psum_tensor(f"pw{i}", [128, 1024], F32))
                   for i in range(4)]

        consts = [
            ("gpsimd", self.ones128_bf[:, :], self.ones128_d, "c_ones128bf"),
            ("sync", self.ones128_f[:, :], self.ones128_d, "c_ones128f"),
            ("sync", self.ones1_f[:, :], self.ones1_d, "c_ones1"),
            ("sync", self.ident[:, :], self.ident_d, "c_ident"),
            ("gpsimd", self.gwt_bf[:, :], self.gwt, "c_gwt"),
            ("sync", self.bq_sb[:, :], self.bqm, "c_bq"),
            ("sync", self.bk_sb[:, :], self.bkm, "c_bk"),
            ("sync", self.bo_sb[:, :], self.bom, "c_bo"),
        ]
        for i, (eng, dst, src, key) in enumerate(consts):
            self.dma(eng, dst, src, f"d_c{i}", [], [key], f"ld_{key}")

        self.ffn_phase(phase=1)
        if cfg.get("do_p2", True):
            B.barrier()
            self.reset_slab()
            self.attn_phase_b()
            B.barrier()
            self.reset_slab()
            self.attn_phase_c()
        if cfg.get("do_p3", True):
            B.barrier()
            self.reset_slab()
            self.ffn_phase(phase=3)
        else:
            B.barrier()
            src, key = ((self.xhat2, "d_xhat2") if cfg.get("do_p2", True)
                        else (self.xhat1, "d_xhat1"))
            self.dma("sync", self.out_fm, src, "d_out", [key], ["d_out_fm"], "dbg_out")

        B.resolve()
        fin = [(k, v) for k, v in B.max_cums.items() if k.startswith("d_out") and v > 0]
        self.sems = B.playback(nc, self.ctx, fin)
        self.ctx.close()
        return nc

    def ffn_phase(self, phase):
        cfg = self.cfg
        self.reset_slab()
        c = self.carve
        self.stats = c("strow", [1, 2048], F32)
        bufs = dict(
            xbuf=c("xbuf", [128, KD * T], F32),
            xnbuf=c("xnbuf", [128, KD * T], BF16),
            hbuf=c("hbuf", [128, NI * T], BF16),
            w12sl=[c(f"w12_{i}", [128, 2 * D], BF16) for i in range(2)],
            w3sl=[c(f"w3_{i}", [128, 2 * I], BF16) for i in range(2)],
            tr=[c(f"tr{i}", [128, T], F32) for i in range(4)],
            sqb=[c(f"sq{i}", [128, T], BF16) for i in range(2)],
            combb=c("combb", [128, T], F32),
            meanb=c("meanb", [128, T], F32),
            rstdb=c("rstdb", [128, T], F32),
            lg=c("lg", [128, 64], F32),
            exs=c("exs", [128, 64], F32),
            pp=c("pp", [128, 64], F32),
            eq=c("eq", [128, 64], F32),
            keep=c("keep", [128, 64], F32),
            comb=c("comb", [128, 64], F32),
            mx=c("mx", [128, 8], F32),
            sm=c("sm", [128, 8], F32),
            combT=c("combT", [64, 128], F32),
            combe=[c(f"combe{i}", [1, T], F32) for i in range(2)],
        )
        first = (phase == 1)
        n_pass = cfg.get("passes", NPASS)
        for p in range(n_pass):
            pfx = f"p{phase}.{p}"
            tsl = slice(p * T, (p + 1) * T)
            use_cache = (not first) or (p > 0)
            if first:
                for j in range(4):
                    self.dma("sync",
                             bufs["xbuf"][:, j * 4 * T:(j + 1) * 4 * T].rearrange("q (k t) -> q k t", k=4),
                             self.x_fm.rearrange("(k q) t -> q k t", q=128)[:, 4 * j:4 * (j + 1), tsl],
                             f"d_xin{j}", ["d_x_fm"],
                             [f"xbuf.{4 * j + i}" for i in range(4)], f"{pfx}.ldx{j}")
                self.emit_ln(bufs, src_key="xbuf", pfx=f"{pfx}.ln1")
            else:
                for j in range(4):
                    self.dma("sync",
                             bufs["xnbuf"][:, j * 4 * T:(j + 1) * 4 * T].rearrange("q (k t) -> q k t", k=4),
                             self.n2.rearrange("(k q) t -> q k t", q=128)[:, 4 * j:4 * (j + 1), tsl],
                             f"d_xin{j}", ["d_n2"],
                             [f"xnbuf.{4 * j + i}" for i in range(4)], f"{pfx}.ldxn{j}")
            self.emit_gate(bufs, pfx=f"{pfx}.g")
            jobs = []
            if cfg.get("shared", True):
                jobs += [("s", 0), ("s", 1)]
            jobs += [("e", e) for e in range(cfg.get("experts", E))]
            for jn, (kind, e) in enumerate(jobs):
                self.emit_expert(kind, e, jn, bufs, use_cache,
                                 write_cache=(first and p == 0), pfx=pfx)
            self.emit_epilogue(phase, p, tsl, bufs)

    def emit_ln(self, bufs, src_key, pfx):
        B = self.B
        st = self.stats
        xf, xb, sqb, tr = bufs["xbuf"], bufs["xnbuf"], bufs["sqb"], bufs["tr"]
        meanb, rstdb = bufs["meanb"], bufs["rstdb"]
        pw = self.pw
        for k in range(KD):
            B.op("vector",
                 lambda v, k=k, xb=xb, xf=xf: v.tensor_copy(
                     xb[:, k * T:(k + 1) * T], xf[:, k * T:(k + 1) * T]),
                 reads=[f"{src_key}.{k}"], writes=[f"xnbuf.{k}"], tag=f"{pfx}.cp{k}")
            s = sqb[k % 2]
            B.op("scalar",
                 lambda sc, k=k, s=s, xb=xb: sc.activation(
                     s[:, :], xb[:, k * T:(k + 1) * T], AF.Square),
                 reads=[f"xnbuf.{k}"], writes=[f"sq{k % 2}"], tag=f"{pfx}.sq{k}")
            plan0 = [(pw[0][0:1, h * 512:(h + 1) * 512], self.ones128_bf[:, :],
                      xb[:, k * T + h * 512:k * T + (h + 1) * 512], k == 0, k == KD - 1)
                     for h in range(2)]
            self.mm_group(plan0, reads=[f"xnbuf.{k}", "c_ones128bf"],
                          writes=["pw0a", "pw0b"], tag=f"{pfx}.msm{k}")
            plan1 = [(pw[1][0:1, h * 512:(h + 1) * 512], self.ones128_bf[:, :],
                      s[:, h * 512:(h + 1) * 512], k == 0, k == KD - 1)
                     for h in range(2)]
            self.mm_group(plan1, reads=[f"sq{k % 2}", "c_ones128bf"],
                          writes=["pw1a", "pw1b"], tag=f"{pfx}.msq{k}")
        mean = st[0:1, 0:T]
        msq = st[0:1, T:2 * T]
        B.op("vector", lambda v, mean=mean: v.tensor_scalar_mul(mean[:, :], self.pw[0][0:1, 0:T], 1.0 / D),
             reads=["pw0a", "pw0b"], writes=["r_mean"], tag=f"{pfx}.mean")
        B.op("vector", lambda v, msq=msq: v.tensor_scalar_mul(msq[:, :], self.pw[1][0:1, 0:T], 1.0 / D),
             reads=["pw1a", "pw1b"], writes=["r_msq"], tag=f"{pfx}.msq")
        B.op("vector", lambda v, mean=mean: v.tensor_mul(self.pw[2][0:1, 0:T], mean[:, :], mean[:, :]),
             reads=["r_mean"], writes=["pw2a", "pw2b"], tag=f"{pfx}.m2")
        B.op("vector", lambda v, msq=msq: v.tensor_sub(msq[:, :], msq[:, :], self.pw[2][0:1, 0:T]),
             reads=["r_msq", "pw2a", "pw2b"], writes=["r_msq"], tag=f"{pfx}.var")
        B.op("vector", lambda v, msq=msq: v.tensor_scalar_add(msq[:, :], msq[:, :], EPS),
             reads=["r_msq"], writes=["r_msq"], tag=f"{pfx}.addeps")
        B.op("scalar", lambda s, msq=msq: s.activation(msq[:, :], msq[:, :], AF.Sqrt),
             reads=["r_msq"], writes=["r_msq"], tag=f"{pfx}.sqrt")
        B.op("vector", lambda v, msq=msq: v.reciprocal(msq[:, :], msq[:, :]),
             reads=["r_msq"], writes=["r_msq"], tag=f"{pfx}.recip")
        for h in range(2):
            self.mm_group([(pw[2][:, h * 512:(h + 1) * 512], self.ones1_f[:, :],
                            mean[:, h * 512:(h + 1) * 512], True, True)],
                          reads=["r_mean", "c_ones1"], writes=[f"pw2{'ab'[h]}"], tag=f"{pfx}.bm{h}")
            self.mm_group([(pw[3][:, h * 512:(h + 1) * 512], self.ones1_f[:, :],
                            msq[:, h * 512:(h + 1) * 512], True, True)],
                          reads=["r_msq", "c_ones1"], writes=[f"pw3{'ab'[h]}"], tag=f"{pfx}.br{h}")
        B.op("vector", lambda v, meanb=meanb: v.tensor_copy(meanb[:, :], self.pw[2][:, 0:T]),
             reads=["pw2a", "pw2b"], writes=["meanb"], tag=f"{pfx}.cbm")
        B.op("vector", lambda v, rstdb=rstdb: v.tensor_copy(rstdb[:, :], self.pw[3][:, 0:T]),
             reads=["pw3a", "pw3b"], writes=["rstdb"], tag=f"{pfx}.cbr")
        for k in range(KD):
            t = tr[k % 2]
            B.op("vector", lambda v, k=k, t=t, xf=xf, meanb=meanb: v.tensor_sub(
                 t[:, :], xf[:, k * T:(k + 1) * T], meanb[:, :]),
                 reads=[f"{src_key}.{k}", "meanb"], writes=[f"tr{k % 2}"], tag=f"{pfx}.sub{k}")
            B.op("vector", lambda v, k=k, t=t, xb=xb, rstdb=rstdb: v.tensor_mul(
                 xb[:, k * T:(k + 1) * T], t[:, :], rstdb[:, :]),
                 reads=[f"tr{k % 2}", "rstdb"], writes=[f"xnbuf.{k}"], tag=f"{pfx}.mul{k}")
    def emit_gate(self, bufs, pfx):
        B = self.B
        pw = self.pw
        NTC = T // 128
        xnbuf = bufs["xnbuf"]
        lg, exs, pp, eq, keep, comb = (bufs[k] for k in ("lg", "exs", "pp", "eq", "keep", "comb"))
        mx, sm, combT = bufs["mx"], bufs["sm"], bufs["combT"]
        plan = []
        for tc in range(NTC):
            for k in range(KD):
                plan.append((pw[0][:, tc * 8:(tc + 1) * 8],
                             xnbuf[:, k * T + tc * 128:k * T + (tc + 1) * 128],
                             self.gwt_bf[:, k * 8:(k + 1) * 8],
                             k == 0, k == KD - 1))
        self.mm_group(plan, reads=[f"xnbuf.{_k}" for _k in range(KD)] + ["c_gwt"], writes=["pw0a"], tag=f"{pfx}.lg")
        v3 = lambda ap: ap.rearrange("q (tc e) -> q tc e", e=E)
        r3 = lambda ap: ap.rearrange("q (tc o) -> q tc o", o=1)
        bc = lambda ap: r3(ap).broadcast_to((128, NTC, E))
        B.op("vector", lambda v: v.tensor_copy(lg[:, :], pw[0][:, 0:64]),
             reads=["pw0a"], writes=["lg"], tag=f"{pfx}.cp")
        B.op("vector", lambda v: v.reduce_max(r3(mx[:, :]), v3(lg[:, :]), axis=AX.X),
             reads=["lg"], writes=["mx"], tag=f"{pfx}.mx")
        B.op("vector", lambda v: v.tensor_tensor(v3(exs[:, :]), v3(lg[:, :]), bc(mx[:, :]), ALU.subtract),
             reads=["lg", "mx"], writes=["exs"], tag=f"{pfx}.subm")
        B.op("scalar", lambda s: s.activation(exs[:, :], exs[:, :], AF.Exp),
             reads=["exs"], writes=["exs"], tag=f"{pfx}.exp")
        B.op("vector", lambda v: v.reduce_sum(r3(sm[:, :]), v3(exs[:, :]), axis=AX.X),
             reads=["exs"], writes=["sm"], tag=f"{pfx}.sm")
        B.op("vector", lambda v: v.reciprocal(sm[:, :], sm[:, :]),
             reads=["sm"], writes=["sm"], tag=f"{pfx}.rc")
        B.op("vector", lambda v: v.tensor_tensor(v3(pp[:, :]), v3(exs[:, :]), bc(sm[:, :]), ALU.mult),
             reads=["exs", "sm"], writes=["pp"], tag=f"{pfx}.p")
        B.op("vector", lambda v: v.reduce_max(r3(mx[:, :]), v3(pp[:, :]), axis=AX.X),
             reads=["pp"], writes=["mx"], tag=f"{pfx}.m1")
        B.op("vector", lambda v: v.tensor_tensor(v3(eq[:, :]), v3(pp[:, :]), bc(mx[:, :]), ALU.is_ge),
             reads=["pp", "mx"], writes=["eq"], tag=f"{pfx}.eq")
        B.op("vector", lambda v: v.scalar_tensor_tensor(eq[:, :], eq[:, :], -2.0, pp[:, :], ALU.mult, ALU.add),
             reads=["eq", "pp"], writes=["eq"], tag=f"{pfx}.pm")
        B.op("vector", lambda v: v.reduce_max(r3(mx[:, :]), v3(eq[:, :]), axis=AX.X),
             reads=["eq"], writes=["mx"], tag=f"{pfx}.m2")
        B.op("vector", lambda v: v.tensor_tensor(v3(keep[:, :]), v3(pp[:, :]), bc(mx[:, :]), ALU.is_ge),
             reads=["pp", "mx"], writes=["keep"], tag=f"{pfx}.keep")
        B.op("vector", lambda v: v.tensor_mul(comb[:, :], pp[:, :], keep[:, :]),
             reads=["pp", "keep"], writes=["comb"], tag=f"{pfx}.comb")
        B.op("tensor", lambda t: t.transpose(pw[1][0:64, 0:128], comb[:, :], self.ident[:, :]),
             reads=["comb", "c_ident"], writes=["pw1a"], tag=f"{pfx}.tr")
        B.op("vector", lambda v: v.tensor_copy(combT[:, :], pw[1][0:64, 0:128]),
             reads=["pw1a"], writes=["combT"], tag=f"{pfx}.cpT")
        self.dma("scalar", self.cmb_scr, combT[:, :], "d_cmb", ["combT"], ["d_cmb_scr"], f"{pfx}.st")
    def emit_expert(self, kind, e, jn, bufs, use_cache, write_cache, pfx):
        B = self.B
        pw = self.pw
        pfx = f"{pfx}.x{kind}{e}"
        xnbuf, hbuf, xbuf, combb = (bufs[k] for k in ("xnbuf", "hbuf", "xbuf", "combb"))
        tr = bufs["tr"]
        if kind == "e":
            ce = bufs["combe"][e % 2]
            cekey = f"combe{e % 2}"
            self.dma("sync", ce[0:1, :].rearrange("q (tc j) -> q tc j", tc=8),
                     self.cmb_scr.rearrange("(tc e) j -> e tc j", tc=8, e=E)[e:e + 1, :, :],
                     f"d_ce{e % 2}", ["d_cmb_scr"], [cekey], f"{pfx}.ldce")
            for h in range(2):
                self.mm_group([(pw[2][:, h * 512:(h + 1) * 512], self.ones1_f[:, :],
                                ce[0:1, h * 512:(h + 1) * 512], True, True)],
                              reads=[cekey, "c_ones1"], writes=[f"pw2{'ab'[h]}"], tag=f"{pfx}.cb{h}")
            B.op("vector", lambda v, combb=combb: v.tensor_copy(combb[:, :], self.pw[2][:, 0:T]),
                 reads=["pw2a", "pw2b"], writes=["combb"], tag=f"{pfx}.cbc")

        for i in range(NI):
            sl = bufs["w12sl"][i % 2]
            slot = f"w12_{i % 2}"
            if kind == "e":
                src_f = self.w12[e, :, i * 2 * D:(i + 1) * 2 * D]
                src_c = self.w12c[e, :, i * 2 * D:(i + 1) * 2 * D]
            else:
                ii = e * NI + i
                src_f = self.w12s[:, ii * 2 * D:(ii + 1) * 2 * D]
                src_c = self.w12sc[:, ii * 2 * D:(ii + 1) * 2 * D]
            ckey = f"wc12.{kind}{e}.{i}"
            if use_cache:
                self.dma("gpsimd", sl[:, :], src_c, f"d_{slot}", [ckey], [slot], f"{pfx}.wl{i}")
            else:
                self.dma("gpsimd", sl[:, :], src_f, f"d_{slot}", [], [slot], f"{pfx}.wl{i}")
                if write_cache:
                    self.dma("scalar", src_c, sl[:, :], f"d_{slot}", [slot], [ckey], f"{pfx}.ws{i}")
            for which in range(2):
                plan = []
                for h in range(2):
                    for k in range(KD):
                        plan.append((pw[2 + which][:, h * 512:(h + 1) * 512],
                                     sl[:, which * D + k * 128:which * D + (k + 1) * 128],
                                     xnbuf[:, k * T + h * 512:k * T + (h + 1) * 512],
                                     k == 0, k == KD - 1))
                self.mm_group(plan, reads=[slot] + [f"xnbuf.{_k}" for _k in range(KD)],
                              writes=[f"pw{2 + which}a", f"pw{2 + which}b"],
                              tag=f"{pfx}.h{which}.{i}")
            sil = tr[i % 2]
            ht = tr[2 + i % 2]
            B.op("scalar", lambda s, sil=sil: s.activation(sil[:, :], self.pw[2][:, 0:T], AF.Silu),
                 reads=["pw2a", "pw2b"], writes=[f"tr{i % 2}"], tag=f"{pfx}.sa{i}")
            B.op("vector", lambda v, sil=sil, ht=ht: v.tensor_mul(ht[:, :], sil[:, :], self.pw[3][:, 0:T]),
                 reads=[f"tr{i % 2}", "pw3a", "pw3b"], writes=[f"tr{2 + i % 2}"], tag=f"{pfx}.hm{i}")
            if kind == "e":
                B.op("vector", lambda v, ht=ht, i=i, hbuf=hbuf, combb=combb: v.tensor_mul(
                     hbuf[:, i * T:(i + 1) * T], ht[:, :], combb[:, :]),
                     reads=[f"tr{2 + i % 2}", "combb"], writes=["hbuf"], tag=f"{pfx}.hc{i}")
            else:
                B.op("vector", lambda v, ht=ht, i=i, hbuf=hbuf: v.tensor_copy(
                     hbuf[:, i * T:(i + 1) * T], ht[:, :]),
                     reads=[f"tr{2 + i % 2}"], writes=["hbuf"], tag=f"{pfx}.hp{i}")

        for dp in range(KD // 2):
            sl = bufs["w3sl"][dp % 2]
            slot = f"w3_{dp % 2}"
            ckey = f"wc3.{kind}{e}.{dp}"
            if kind == "e":
                src_f = self.w3b[e, :, dp * 2 * I:(dp + 1) * 2 * I]
                src_c = self.w3c[e, :, dp * 2 * I:(dp + 1) * 2 * I]
                if use_cache:
                    self.dma("gpsimd", sl[:, :], src_c, f"d_{slot}", [ckey], [slot], f"{pfx}.w3l{dp}")
                else:
                    self.dma("gpsimd", sl[:, :], src_f, f"d_{slot}", [], [slot], f"{pfx}.w3l{dp}")
                    if write_cache:
                        self.dma("scalar", src_c, sl[:, :], f"d_{slot}", [slot], [ckey], f"{pfx}.w3s{dp}")
            else:
                big = self.w3sc if use_cache else self.w3s
                srcv = big.rearrange("q (d ki j) -> q d ki j", d=KD, ki=NIS)[
                    :, 2 * dp:2 * dp + 2, e * NI:(e + 1) * NI, :]
                dstv = sl[:, :].rearrange("q (d ki j) -> q d ki j", d=2, ki=NI)
                if use_cache:
                    self.dma("gpsimd", dstv, srcv, f"d_{slot}", [ckey], [slot], f"{pfx}.w3l{dp}")
                else:
                    self.dma("gpsimd", dstv, srcv, f"d_{slot}", [], [slot], f"{pfx}.w3l{dp}")
                    if write_cache:
                        dstc = self.w3sc.rearrange("q (d ki j) -> q d ki j", d=KD, ki=NIS)[
                            :, 2 * dp:2 * dp + 2, e * NI:(e + 1) * NI, :]
                        self.dma("scalar", dstc, dstv, f"d_{slot}", [slot], [ckey], f"{pfx}.w3s{dp}")
            for dd in range(2):
                d = dp * 2 + dd
                plan = []
                for h in range(2):
                    for ki in range(NI):
                        plan.append((pw[dd][:, h * 512:(h + 1) * 512],
                                     sl[:, dd * I + ki * 128:dd * I + (ki + 1) * 128],
                                     hbuf[:, ki * T + h * 512:ki * T + (h + 1) * 512],
                                     ki == 0, ki == NI - 1))
                self.mm_group(plan, reads=[slot, "hbuf"],
                              writes=[f"pw{dd}a", f"pw{dd}b"], tag=f"{pfx}.w3d{d}")
                if jn == 0:
                    B.op("vector", lambda v, d=d, dd=dd, xbuf=xbuf: v.tensor_copy(
                         xbuf[:, d * T:(d + 1) * T], self.pw[dd][:, 0:T]),
                         reads=[f"pw{dd}a", f"pw{dd}b"], writes=[f"xbuf.{d}"], tag=f"{pfx}.ac{d}")
                else:
                    B.op("vector", lambda v, d=d, dd=dd, xbuf=xbuf: v.tensor_add(
                         xbuf[:, d * T:(d + 1) * T], xbuf[:, d * T:(d + 1) * T], self.pw[dd][:, 0:T]),
                         reads=[f"pw{dd}a", f"pw{dd}b", f"xbuf.{d}"], writes=[f"xbuf.{d}"], tag=f"{pfx}.aa{d}")
    def emit_epilogue(self, phase, p, tsl, bufs):
        B = self.B
        pfx = f"p{phase}.{p}.ep"
        first = (phase == 1)
        xbuf, xnbuf, tr = bufs["xbuf"], bufs["xnbuf"], bufs["tr"]
        src = self.x_fm if first else self.xhat2
        srckey = "d_x_fm" if first else "d_xhat2"
        for k in range(KD):
            xe = tr[k % 2]
            self.dma("sync", xe[:, :],
                     src.rearrange("(k q) t -> q k t", q=128)[:, k, tsl.start:tsl.stop],
                     f"d_xep{k % 2}", [srckey], [f"tr{k % 2}"], f"{pfx}.ldx{k}")
            B.op("vector",
                 lambda v, k=k, xe=xe, xbuf=xbuf: v.scalar_tensor_tensor(
                     xbuf[:, k * T:(k + 1) * T], xbuf[:, k * T:(k + 1) * T], 0.5,
                     xe[:, :], ALU.mult, ALU.add),
                 reads=[f"xbuf.{k}", f"tr{k % 2}"], writes=[f"xbuf.{k}"], tag=f"{pfx}.x1{k}")
        if first:
            for j in range(4):
                self.dma("scalar",
                         self.xhat1.rearrange("(k q) t -> q k t", q=128)[:, 4 * j:4 * (j + 1), tsl],
                         xbuf[:, j * 4 * T:(j + 1) * 4 * T].rearrange("q (k t) -> q k t", k=4),
                         f"d_x1st{j}", [f"xbuf.{4 * j + i}" for i in range(4)],
                         ["d_xhat1"], f"{pfx}.stx1{j}")
            self.emit_ln(bufs, src_key="xbuf", pfx=f"{pfx}.ln2")
            for j in range(4):
                self.dma("scalar",
                         self.n1.rearrange("(k q) t -> q k t", q=128)[:, 4 * j:4 * (j + 1), tsl],
                         xnbuf[:, j * 4 * T:(j + 1) * 4 * T].rearrange("q (k t) -> q k t", k=4),
                         f"d_n1st{j}", [f"xnbuf.{4 * j + i}" for i in range(4)],
                         ["d_n1"], f"{pfx}.stn1{j}")
        else:
            for j in range(4):
                self.dma("scalar",
                         self.out_fm.rearrange("(k q) t -> q k t", q=128)[:, 4 * j:4 * (j + 1), tsl],
                         xbuf[:, j * 4 * T:(j + 1) * 4 * T].rearrange("q (k t) -> q k t", k=4),
                         f"d_out{j}", [f"xbuf.{4 * j + i}" for i in range(4)],
                         ["d_out_fm"], f"{pfx}.stout{j}")

    def attn_phase_b(self):
        B = self.B
        c = self.carve
        pw = self.pw
        self.stats = c("strow2", [1, 1152], F32)
        st = self.stats
        self.bvr_sb = c("bvr_sb", [1, D], F32)
        self.dma("sync", self.bvr_sb[:, :], self.bvr, "d_cbv", [], ["c_bv"], "ld_c_bv")
        xn1 = c("xn1", [128, KD * S], BF16)
        qb = c("q_bf", [128, S], BF16)
        kb = c("k_bf", [128, S], BF16)
        vb = c("v_bf", [128, S], BF16)
        ab = c("att_h", [128, S], BF16)
        wsl = [c(f"aw{i}", [128, KD * 128], BF16) for i in range(3)]
        exb = [c(f"expT{i}", [128, 1024], BF16) for i in range(2)]
        rcb = [c(f"rcb{i}", [128, 512], F32) for i in range(2)]
        sqt = c("sqt", [128, 1024], BF16)
        bias_c = c("bias_c", [128, 4], F32)
        bvb = c("bvb", [128, 128], F32)
        for j in range(4):
            self.dma("sync",
                     xn1[:, :].rearrange("q (k t) -> q k t", k=KD)[:, :, j * 1024:(j + 1) * 1024],
                     self.n1.rearrange("(k q) t -> q k t", q=128)[:, :, j * 1024:(j + 1) * 1024],
                     f"d_xn1{j}", ["d_n1"], [f"xn1.{j}"], f"p2.ldxn1{j}")

        def load_w(h, pfx):
            for (i, (dsrc, nm)) in enumerate(((self.wqb, "q"), (self.wkb, "k"), (self.wvb, "v"))):
                self.dma("gpsimd", wsl[i][:, :], dsrc[:, h * D:(h + 1) * D],
                         f"d_aw{nm}", [], [f"aw{nm}"], f"{pfx}.w{nm}")

        n_heads = self.cfg.get("heads", H)
        load_w(0, "p2.h0pre")
        for h in range(n_heads):
            pfx = f"p2.h{h}"
            wq, wk, wv = wsl
            self.mm_group([(pw[3][:, 0:128], self.ones1_f[:, :],
                            self.bvr_sb[0:1, h * 128:(h + 1) * 128], True, True)],
                          reads=["c_bv", "c_ones1"], writes=["pw3a"], tag=f"{pfx}.bvb")
            B.op("vector", lambda v, bvb=bvb: v.tensor_copy(bvb[:, :], self.pw[3][:, 0:128]),
                 reads=["pw3a"], writes=["bvb"], tag=f"{pfx}.bvc")
            for (w, bmat, dst, dkey, nm) in ((wq, self.bq_sb, qb, "qbf", "q"),
                                             (wk, self.bk_sb, kb, "kbf", "k")):
                wkey = f"aw{nm}"
                for tq in range(8):
                    hp = tq % 2
                    plan = [(pw[2][:, hp * 512:(hp + 1) * 512], w[:, k * 128:(k + 1) * 128],
                             xn1[:, k * S + tq * 512:k * S + (tq + 1) * 512],
                             k == 0, k == KD - 1) for k in range(KD)]
                    self.mm_group(plan, reads=[wkey, f"xn1.{tq // 2}"], writes=[f"pw2{'ab'[hp]}"], tag=f"{pfx}.{nm}m{tq}")
                    B.op("scalar", lambda s, tq=tq, hp=hp, dst=dst, bmat=bmat, h=h: s.add(
                         dst[:, tq * 512:(tq + 1) * 512], self.pw[2][:, hp * 512:(hp + 1) * 512],
                         bmat[:, h:h + 1]),
                         reads=[f"pw2{'ab'[hp]}", "c_b" + nm], writes=[dkey], tag=f"{pfx}.{nm}e{tq}")
            for tc in range(32):
                hp = tc % 2
                plan = [(pw[3][:, hp * 512:hp * 512 + 128],
                         xn1[:, k * S + tc * 128:k * S + (tc + 1) * 128],
                         wv[:, k * 128:(k + 1) * 128],
                         k == 0, k == KD - 1) for k in range(KD)]
                self.mm_group(plan, reads=["awv", f"xn1.{tc // 8}"], writes=[f"pw3{'ab'[hp]}"], tag=f"{pfx}.vm{tc}")
                B.op("vector", lambda v, tc=tc, hp=hp, vb=vb, bvb=bvb: v.tensor_tensor(
                     vb[:, tc * 128:(tc + 1) * 128], self.pw[3][:, hp * 512:hp * 512 + 128],
                     bvb[:, :], ALU.add),
                     reads=[f"pw3{'ab'[hp]}", "bvb"], writes=["vbf"], tag=f"{pfx}.ve{tc}")
            if h + 1 < n_heads:
                load_w(h + 1, f"p2.h{h + 1}pre")
            for o, (buf, bkey) in enumerate(((qb, "qbf"), (kb, "kbf"))):
                for tq in range(8):
                    i = o * 8 + tq
                    par = i % 2
                    sq_half = sqt[:, par * 512:(par + 1) * 512]
                    if o == 0:
                        B.op("scalar", lambda s, buf=buf, tq=tq, sq_half=sq_half: s.activation(
                             sq_half, buf[:, tq * 512:(tq + 1) * 512], AF.Square),
                             reads=[bkey], writes=[f"sqt{par}"], tag=f"{pfx}.n{o}.{tq}")
                    else:
                        B.op("vector", lambda v, buf=buf, tq=tq, sq_half=sq_half: v.tensor_mul(
                             sq_half, buf[:, tq * 512:(tq + 1) * 512], buf[:, tq * 512:(tq + 1) * 512]),
                             reads=[bkey], writes=[f"sqt{par}"], tag=f"{pfx}.n{o}.{tq}")
                    self.mm_group([(pw[2][0:1, par * 512:par * 512 + 512], self.ones128_bf[:, :],
                                    sq_half, True, True)],
                                  reads=[f"sqt{par}", "c_ones128bf"], writes=[f"pw2{'ab'[par]}"],
                                  tag=f"{pfx}.nm{o}.{tq}")
                    B.op("vector", lambda v, o=o, tq=tq, par=par: v.reduce_max(
                         st[0:1, o * 8 + tq:o * 8 + tq + 1],
                         self.pw[2][0:1, par * 512:par * 512 + 512], axis=AX.X),
                         reads=[f"pw2{'ab'[par]}"], writes=["r_nrm"], tag=f"{pfx}.nx{o}.{tq}")
            B.op("vector", lambda v: v.reduce_max(st[0:1, 16:17], st[0:1, 0:8], axis=AX.X),
                 reads=["r_nrm"], writes=["r_nrm"], tag=f"{pfx}.qm")
            B.op("vector", lambda v: v.reduce_max(st[0:1, 17:18], st[0:1, 8:16], axis=AX.X),
                 reads=["r_nrm"], writes=["r_nrm"], tag=f"{pfx}.km")
            B.op("vector", lambda v: v.tensor_add(st[0:1, 18:19], st[0:1, 16:17], st[0:1, 17:18]),
                 reads=["r_nrm"], writes=["r_nrm"], tag=f"{pfx}.sm2")
            B.op("vector", lambda v: v.tensor_scalar_mul(st[0:1, 19:20], st[0:1, 18:19], -0.5 * SCALE),
                 reads=["r_nrm"], writes=["r_nrm"], tag=f"{pfx}.ng")
            self.mm_group([(pw[2][:, 0:4], self.ones1_f[:, :], st[0:1, 16:20], True, True)],
                          reads=["r_nrm", "c_ones1"], writes=["pw2a"], tag=f"{pfx}.cb")
            B.op("vector", lambda v, bias_c=bias_c: v.tensor_copy(bias_c[:, :], self.pw[2][:, 0:4]),
                 reads=["pw2a"], writes=["biasc"], tag=f"{pfx}.cbc")
            pending_norm = None
            for qc in range(8):
                qp = qc % 2
                for it in range(17):
                    if it < 16:
                        pp = it % 2
                        plan_s = [(pw[pp][:, hh * 512:(hh + 1) * 512],
                                   kb[:, (2 * it + hh) * 128:(2 * it + hh + 1) * 128],
                                   qb[:, qc * 512:(qc + 1) * 512], True, True)
                                  for hh in range(2)]
                        self.mm_group(plan_s, reads=["kbf", "qbf"],
                                      writes=[f"pw{pp}a", f"pw{pp}b"], tag=f"{pfx}.s{qc}.{it}")
                    if it == 0 and pending_norm is not None:
                        pending_norm()
                        pending_norm = None
                    if it >= 1:
                        pr = it - 1
                        pp = pr % 2
                        B.op("scalar", lambda s, pp=pp, exb=exb, bias_c=bias_c: s.activation(
                             exb[pp][:, :], self.pw[pp][:, 0:1024], AF.Exp,
                             bias=bias_c[:, 3:4], scale=SCALE),
                             reads=[f"pw{pp}a", f"pw{pp}b", "biasc"], writes=[f"expT{pp}"],
                             tag=f"{pfx}.e{qc}.{pr}")
                        plan_pv = []
                        plan_dn = []
                        for s2 in range(2):
                            kk2 = 2 * pr + s2
                            plan_pv.append((pw[2][:, qp * 512:(qp + 1) * 512],
                                            vb[:, kk2 * 128:(kk2 + 1) * 128],
                                            exb[pp][:, s2 * 512:(s2 + 1) * 512],
                                            kk2 == 0, kk2 == 31))
                            plan_dn.append((pw[3][0:1, qp * 512:(qp + 1) * 512],
                                            self.ones128_bf[:, :],
                                            exb[pp][:, s2 * 512:(s2 + 1) * 512],
                                            kk2 == 0, kk2 == 31))
                        self.mm_group(plan_dn, reads=[f"expT{pp}", "c_ones128bf"],
                                      writes=[f"pw3{'ab'[qp]}"], tag=f"{pfx}.d{qc}.{pr}")
                        self.mm_group(plan_pv, reads=[f"expT{pp}", "vbf"],
                                      writes=[f"pw2{'ab'[qp]}"], tag=f"{pfx}.o{qc}.{pr}")
                def emit_norm(qc=qc, qp=qp):
                    B.op("vector", lambda v, qp=qp: v.reciprocal(
                         st[0:1, 128 + qp * 512:128 + (qp + 1) * 512], self.pw[3][0:1, qp * 512:(qp + 1) * 512]),
                         reads=[f"pw3{'ab'[qp]}"], writes=[f"r_den{qp}"], tag=f"{pfx}.rc{qc}")
                    self.mm_group([(pw[3][:, qp * 512:(qp + 1) * 512], self.ones1_f[:, :],
                                    st[0:1, 128 + qp * 512:128 + (qp + 1) * 512], True, True)],
                                  reads=[f"r_den{qp}", "c_ones1"], writes=[f"pw3{'ab'[qp]}"], tag=f"{pfx}.rb{qc}")
                    B.op("vector", lambda v, qp=qp, rcb=rcb: v.tensor_copy(
                         rcb[qp][:, :], self.pw[3][:, qp * 512:(qp + 1) * 512]),
                         reads=[f"pw3{'ab'[qp]}"], writes=[f"rcbx{qp}"], tag=f"{pfx}.rv{qc}")
                    B.op("vector", lambda v, qc=qc, qp=qp, ab=ab, rcb=rcb: v.tensor_tensor(
                         ab[:, qc * 512:(qc + 1) * 512], self.pw[2][:, qp * 512:(qp + 1) * 512],
                         rcb[qp][:, :], ALU.mult),
                         reads=[f"pw2{'ab'[qp]}", f"rcbx{qp}"], writes=["att"], tag=f"{pfx}.at{qc}")
                pending_norm = emit_norm
            pending_norm()
            pending_norm = None
            self.dma("scalar", self.att_s[h * 128:(h + 1) * 128, :], ab[:, :],
                     "d_attst", ["att"], ["d_att_s"], f"{pfx}.st")
    def attn_phase_c(self):
        B = self.B
        c = self.carve
        pw = self.pw
        self.stats = c("strow3", [1, 2048], F32)
        st = self.stats
        wo = c("wo_bf", [128, KD * D], BF16)
        att = [c(f"attt{i}", [128, KD * 512], BF16) for i in range(2)]
        x1t = c("x1t", [128, KD * 512], F32)
        x2t = c("x2t", [128, KD * 512], F32)
        xn2t = c("xn2t", [128, KD * 512], BF16)
        sqt = c("sqt2", [128, 1024], BF16)
        xbt = c("xbt", [128, 1024], BF16)
        tln = c("tln", [128, 512], F32)
        meanb = c("meanb2", [128, 512], F32)
        rstdb = c("rstdb2", [128, 512], F32)
        for j in range(4):
            self.dma("gpsimd", wo[:, j * 4 * D:(j + 1) * 4 * D],
                     self.wob[:, j * 4 * D:(j + 1) * 4 * D],
                     f"d_wo{j}", [], [f"wo.{j}"], f"p2c.ldwo{j}")
        for ts in range(8):
            pfx = f"p2c.t{ts}"
            par = ts % 2
            self.dma("sync", att[par][:, :].rearrange("q (k t) -> q k t", k=KD),
                     self.att_s.rearrange("(k q) t -> q k t", q=128)[:, :, ts * 512:(ts + 1) * 512],
                     f"d_att{par}", ["d_att_s"], [f"attt{par}"], f"{pfx}.la")
            for qh, eng in ((0, "sync"), (1, "scalar")):
                self.dma(eng, x1t[:, qh * 8 * 512:(qh + 1) * 8 * 512].rearrange("q (k t) -> q k t", k=KD // 2),
                         self.xhat1.rearrange("(k q) t -> q k t", q=128)[
                             :, qh * 8:(qh + 1) * 8, ts * 512:(ts + 1) * 512],
                         f"d_x1t{qh}", ["d_xhat1"], ["x1t"], f"{pfx}.lx{qh}")
            stat_pending = None
            for d in range(KD):
                wt_ = pw[(d % 4) // 2]
                hp = d % 2
                pkey = f"pw{(d % 4) // 2}{'ab'[hp]}"
                plan = [(wt_[:, hp * 512:(hp + 1) * 512], wo[:, d * D + k * 128:d * D + (k + 1) * 128],
                         att[par][:, k * 512:(k + 1) * 512], k == 0, k == KD - 1)
                        for k in range(KD)]
                self.mm_group(plan, reads=[f"wo.{d // 4}", f"attt{par}"], writes=[pkey], tag=f"{pfx}.m{d}")
                B.op("vector", lambda v, d=d, wt_=wt_, hp=hp, x2t=x2t, x1t=x1t: v.scalar_tensor_tensor(
                     x2t[:, d * 512:(d + 1) * 512], wt_[:, hp * 512:(hp + 1) * 512], self.bo_sb[:, d:d + 1],
                     x1t[:, d * 512:(d + 1) * 512], ALU.add, ALU.add),
                     reads=[pkey, "x1t", "c_bo"], writes=["x2t"], tag=f"{pfx}.r{d}")
                dp2 = d % 2
                xb_h = xbt[:, dp2 * 512:(dp2 + 1) * 512]
                sq_h = sqt[:, dp2 * 512:(dp2 + 1) * 512]
                B.op("vector", lambda v, d=d, x2t=x2t, xb_h=xb_h: v.tensor_copy(
                     xb_h, x2t[:, d * 512:(d + 1) * 512]),
                     reads=["x2t"], writes=[f"xbt{dp2}"], tag=f"{pfx}.c{d}")
                B.op("scalar", lambda s, sq_h=sq_h, xb_h=xb_h: s.activation(sq_h, xb_h, AF.Square),
                     reads=[f"xbt{dp2}"], writes=[f"sqt2{dp2}"], tag=f"{pfx}.q{d}")
                if stat_pending is not None:
                    stat_pending()

                def emit_stats(d=d, dp2=dp2, xb_h=xb_h, sq_h=sq_h):
                    self.mm_group([(pw[2][0:1, 0:512], self.ones128_bf[:, :], xb_h,
                                    d == 0, d == KD - 1)],
                                  reads=[f"xbt{dp2}", "c_ones128bf"], writes=["pw2a"], tag=f"{pfx}.s{d}")
                    self.mm_group([(pw[2][0:1, 512:1024], self.ones128_bf[:, :], sq_h,
                                    d == 0, d == KD - 1)],
                                  reads=[f"sqt2{dp2}", "c_ones128bf"], writes=["pw2b"], tag=f"{pfx}.z{d}")
                stat_pending = emit_stats
            stat_pending()
            stat_pending = None
            B.op("vector", lambda v: v.tensor_scalar_mul(st[0:1, 0:512], self.pw[2][0:1, 0:512], 1.0 / D),
                 reads=["pw2a"], writes=["r_mean"], tag=f"{pfx}.mn")
            B.op("vector", lambda v: v.tensor_scalar_mul(st[0:1, 512:1024], self.pw[2][0:1, 512:1024], 1.0 / D),
                 reads=["pw2b"], writes=["r_msq"], tag=f"{pfx}.mq")
            B.op("vector", lambda v: v.tensor_mul(self.pw[3][0:1, 0:512], st[0:1, 0:512], st[0:1, 0:512]),
                 reads=["r_mean"], writes=["pw3a"], tag=f"{pfx}.m2")
            B.op("vector", lambda v: v.tensor_sub(st[0:1, 512:1024], st[0:1, 512:1024], self.pw[3][0:1, 0:512]),
                 reads=["r_msq", "pw3a"], writes=["r_msq"], tag=f"{pfx}.va")
            B.op("vector", lambda v: v.tensor_scalar_add(st[0:1, 512:1024], st[0:1, 512:1024], EPS),
                 reads=["r_msq"], writes=["r_msq"], tag=f"{pfx}.ae")
            B.op("scalar", lambda s: s.activation(st[0:1, 512:1024], st[0:1, 512:1024], AF.Sqrt),
                 reads=["r_msq"], writes=["r_msq"], tag=f"{pfx}.sq")
            B.op("vector", lambda v: v.reciprocal(st[0:1, 512:1024], st[0:1, 512:1024]),
                 reads=["r_msq"], writes=["r_msq"], tag=f"{pfx}.rc")
            self.mm_group([(pw[3][:, 0:512], self.ones1_f[:, :], st[0:1, 0:512], True, True)],
                          reads=["r_mean", "c_ones1"], writes=["pw3a"], tag=f"{pfx}.bm")
            self.mm_group([(pw[3][:, 512:1024], self.ones1_f[:, :], st[0:1, 512:1024], True, True)],
                          reads=["r_msq", "c_ones1"], writes=["pw3b"], tag=f"{pfx}.br")
            B.op("vector", lambda v, meanb=meanb: v.tensor_copy(meanb[:, :], self.pw[3][:, 0:512]),
                 reads=["pw3a"], writes=["meanb2"], tag=f"{pfx}.bc")
            B.op("vector", lambda v, rstdb=rstdb: v.tensor_copy(rstdb[:, :], self.pw[3][:, 512:1024]),
                 reads=["pw3b"], writes=["rstdb2"], tag=f"{pfx}.rb")
            for d in range(KD):
                B.op("vector", lambda v, d=d, tln=tln, x2t=x2t, meanb=meanb: v.tensor_sub(
                     tln[:, :], x2t[:, d * 512:(d + 1) * 512], meanb[:, :]),
                     reads=["x2t", "meanb2"], writes=["tln"], tag=f"{pfx}.u{d}")
                B.op("vector", lambda v, d=d, tln=tln, xn2t=xn2t, rstdb=rstdb: v.tensor_mul(
                     xn2t[:, d * 512:(d + 1) * 512], tln[:, :], rstdb[:, :]),
                     reads=["tln", "rstdb2"], writes=["xn2t"], tag=f"{pfx}.w{d}")
            self.dma("gpsimd",
                     self.xhat2.rearrange("(k q) t -> q k t", q=128)[:, :, ts * 512:(ts + 1) * 512],
                     x2t[:, :].rearrange("q (k t) -> q k t", k=KD),
                     "d_x2st", ["x2t"], ["d_xhat2"], f"{pfx}.sx")
            self.dma("scalar",
                     self.n2.rearrange("(k q) t -> q k t", q=128)[:, :, ts * 512:(ts + 1) * 512],
                     xn2t[:, :].rearrange("q (k t) -> q k t", k=KD),
                     "d_n2st", ["xn2t"], ["d_n2"], f"{pfx}.sn")
def build_program(cfg=None):
    mk = MK(cfg or {})
    nc = mk.build()
    return nc


def kernel(**inputs):
    per_core = prep_inputs(inputs)
    nc = build_program({})
    res = run_bass_kernel_spmd(nc, per_core, core_ids=list(range(8)))
    out = np.stack([np.ascontiguousarray(np.asarray(r["out_fm"]).T) for r in res.results])
    return out.astype(np.float32)


# version 06
# speedup vs baseline: 1.1213x; 1.1213x over previous
import numpy as np
from contextlib import ExitStack
import concourse.bass as bass
import concourse.mybir as mybir
from concourse.bass_utils import run_bass_kernel_spmd

F32 = mybir.dt.float32
BF16 = mybir.dt.bfloat16
AF = mybir.ActivationFunctionType
ALU = mybir.AluOpType
AX = mybir.AxisListType

D = 2048
S = 4096
T = 1024
NPASS = S // T
KD = D // 128
E = 8
I = 1408
NI = I // 128
IS = 2816
NIS = IS // 128
H = 16
SCALE = float(1.0 / np.sqrt(128.0))
EPS = 1e-5

ENGINES = ("tensor", "vector", "scalar", "gpsimd", "sync")


class Builder:

    def __init__(self):
        self.ops = []
        self.sem_keys = {"e_" + e for e in ENGINES}

    def op(self, eng, fn, reads=(), writes=(), dma_sem=None, tag="", deps=()):
        if dma_sem is not None:
            self.sem_keys.add(dma_sem)
        self.ops.append(dict(
            eng=eng, fn=fn, reads=tuple(reads), writes=tuple(writes),
            sem=("e_" + eng) if dma_sem is None else dma_sem,
            step=1 if dma_sem is None else 16,
            tag=tag, deps=set(deps), inc=False, cum=0, waits={},
        ))

    def barrier(self):
        last = {}
        for i, o in enumerate(self.ops):
            if o["fn"] is not None:
                last[o["sem"]] = i
        idxs = set(last.values())
        for e in ENGINES:
            self.op(e, None, tag="barrier", deps=idxs)

    def resolve(self):
        last_w = {}
        readers = {}
        for i, o in enumerate(self.ops):
            deps = set(o["deps"])
            for r in o["reads"]:
                j = last_w.get(r)
                if j is not None:
                    deps.add(j)
            for w in o["writes"]:
                j = last_w.get(w)
                if j is not None:
                    deps.add(j)
                deps |= readers.get(w, set())
            o["deps"] = set()
            for j in deps:
                if j == i:
                    continue
                p = self.ops[j]
                if p["eng"] == "tensor" and o["eng"] == "tensor" and p["fn"] is not None and o["fn"] is not None:
                    continue
                o["deps"].add(j)
            for r in o["reads"]:
                readers.setdefault(r, set()).add(i)
            for w in o["writes"]:
                last_w[w] = i
                readers[w] = set()
        for i, o in enumerate(self.ops):
            expand = set()
            for j in o["deps"]:
                if self.ops[j]["fn"] is None:
                    expand |= self.ops[j]["deps"]
            o["deps"] |= expand
            o["deps"] = {j for j in o["deps"] if self.ops[j]["fn"] is not None}
        for o in self.ops:
            if o["step"] == 16 and o["fn"] is not None:
                o["inc"] = True
        for o in self.ops:
            for j in o["deps"]:
                self.ops[j]["inc"] = True
        cums = {k: 0 for k in self.sem_keys}
        for o in self.ops:
            if o["inc"] and o["fn"] is not None:
                cums[o["sem"]] += o["step"]
                o["cum"] = cums[o["sem"]]
        floors = {e: {} for e in ENGINES}
        for o in self.ops:
            waits = {}
            for j in o["deps"]:
                p = self.ops[j]
                assert p["inc"] and p["cum"] > 0, (p["tag"], o["tag"])
                if p["sem"] == o["sem"] and p["eng"] == o["eng"] and p["sem"].startswith("e_"):
                    pass
                fl = floors[o["eng"]].get(p["sem"], 0)
                if p["cum"] > fl:
                    waits[p["sem"]] = max(waits.get(p["sem"], 0), p["cum"])
            for k, v in waits.items():
                floors[o["eng"]][k] = v
            o["waits"] = waits
        self.max_cums = cums

    def playback(self, nc, ctx, dram_final_waits):
        sems = {k: ctx.enter_context(nc.semaphore(k)) for k in sorted(self.sem_keys)}
        block = ctx.enter_context(nc.Block())
        streams = {e: [o for o in self.ops if o["eng"] == e] for e in ENGINES}

        def run_stream(proxy, eng):
            for o in streams[eng]:
                for k, v in sorted(o["waits"].items()):
                    proxy.wait_ge(sems[k], v)
                if o["fn"] is None:
                    continue
                ins = o["fn"](proxy)
                if o["inc"]:
                    assert ins is not None, o["tag"]
                    ins.then_inc(sems[o["sem"]], o["step"])
            if eng == "gpsimd":
                for k, v in dram_final_waits:
                    proxy.wait_ge(sems[k], v)

        @block.tensor
        def _(t):
            run_stream(t, "tensor")

        @block.vector
        def _(v):
            run_stream(v, "vector")

        @block.scalar
        def _(s):
            run_stream(s, "scalar")

        @block.gpsimd
        def _(g):
            run_stream(g, "gpsimd")

        @block.sync
        def _(sy):
            run_stream(sy, "sync")
        return sems



def _blk(w):
    Co, Ci = w.shape
    return np.ascontiguousarray(
        w.reshape(Co // 128, 128, Ci // 128, 128).transpose(3, 0, 2, 1).reshape(128, -1))


def _blk12(w1e, w2e):
    nic = w1e.shape[0] // 128
    a = w1e.reshape(nic, 128, KD, 128)
    b = w2e.reshape(nic, 128, KD, 128)
    s = np.stack([a, b], axis=1)
    return np.ascontiguousarray(s.transpose(4, 0, 1, 3, 2).reshape(128, -1))


def prep_inputs(inp):
    f = lambda a: np.asarray(a, dtype=np.float32)
    w1 = f(inp["L0_ffn_w1"]); w2 = f(inp["L0_ffn_w2"]); w3 = f(inp["L0_ffn_w3"])
    sw1 = f(inp["L0_ffn_sw1"]); sw2 = f(inp["L0_ffn_sw2"]); sw3 = f(inp["L0_ffn_sw3"])
    shared = {
        "w12": np.stack([_blk12(w1[e], w2[e]) for e in range(E)]),
        "w3b": np.stack([_blk(w3[e]) for e in range(E)]),
        "w12s": _blk12(sw1, sw2),
        "w3s": _blk(sw3),
        "gwt": np.ascontiguousarray(
            f(inp["L0_ffn_gate_w"]).T.reshape(KD, 128, E).transpose(1, 0, 2).reshape(128, KD * E)),
        "wqb": _blk(f(inp["L0_attn_wq"])),
        "wkb": _blk(f(inp["L0_attn_wk"])),
        "wvb": _blk(f(inp["L0_attn_wv"])),
        "wob": _blk(f(inp["L0_attn_wo"])),
        "bqm": np.ascontiguousarray(f(inp["L0_attn_bq"]).reshape(KD, 128).T),
        "bkm": np.ascontiguousarray(f(inp["L0_attn_bk"]).reshape(KD, 128).T),
        "bom": np.ascontiguousarray(f(inp["L0_attn_bo"]).reshape(KD, 128).T),
        "bvr": np.ascontiguousarray(f(inp["L0_attn_bv"]).reshape(1, D)),
        "ones128": np.ones((128, 1), np.float32),
        "ones1": np.ones((1, 128), np.float32),
        "ident": np.eye(128, dtype=np.float32),
    }
    x = f(inp["L0_attn_hidden_states"])
    return [dict(shared, x_fm=np.ascontiguousarray(x[b].T)) for b in range(8)]



class MK:
    def __init__(self, cfg):
        self.cfg = cfg
        self.nc = bass.Bass("TRN2", target_bir_lowering=False)
        self.B = Builder()
        self.ctx = ExitStack()
        self.SLAB_BYTES = 205 * 1024
        self._slab_off = 0

    def dram(self, name, shape, dtype, kind=None):
        t = (self.nc.dram_tensor(name, shape, dtype, kind=kind) if kind
             else self.nc.dram_tensor(name, shape, dtype))
        return t.ap()

    def carve(self, name, shape, dtype):
        nbytes = int(np.prod(shape[1:])) * mybir.dt.size(dtype)
        off = (self._slab_off + 31) // 32 * 32
        self._slab_off = off + nbytes
        assert self._slab_off <= self.SLAB_BYTES, (name, self._slab_off)
        ap = self.slab[:, off:off + nbytes].bitcast(dtype)
        return ap[0:shape[0], 0:shape[1]]

    def reset_slab(self):
        self._slab_off = 0

    def mm_group(self, plan, reads, writes, tag=""):
        def fn(t, plan=plan):
            ins = None
            for (o, l, r, st, sp) in plan:
                ins = t.matmul(o, l, r, start=st, stop=sp, skip_group_check=True)
            return ins
        self.B.op("tensor", fn, reads=reads, writes=writes, tag=tag)

    def dma(self, eng, dst, src, sem, reads, writes, tag=""):
        self.B.op(eng, lambda e, dst=dst, src=src: e.dma_start(dst, src),
                  reads=reads, writes=writes, dma_sem=sem, tag=tag)

    def build(self):
        nc, B, cfg = self.nc, self.B, self.cfg
        self.x_fm = self.dram("x_fm", [D, S], F32, kind="ExternalInput")
        self.out_fm = self.dram("out_fm", [D, S], F32, kind="ExternalOutput")
        self.w12 = self.dram("w12", [E, 128, NI * 2 * D], F32, kind="ExternalInput")
        self.w3b = self.dram("w3b", [E, 128, KD * I], F32, kind="ExternalInput")
        self.w12s = self.dram("w12s", [128, NIS * 2 * D], F32, kind="ExternalInput")
        self.w3s = self.dram("w3s", [128, KD * IS], F32, kind="ExternalInput")
        self.gwt = self.dram("gwt", [128, KD * E], F32, kind="ExternalInput")
        self.wqb = self.dram("wqb", [128, H * D], F32, kind="ExternalInput")
        self.wkb = self.dram("wkb", [128, H * D], F32, kind="ExternalInput")
        self.wvb = self.dram("wvb", [128, H * D], F32, kind="ExternalInput")
        self.wob = self.dram("wob", [128, KD * D], F32, kind="ExternalInput")
        self.bqm = self.dram("bqm", [128, KD], F32, kind="ExternalInput")
        self.bkm = self.dram("bkm", [128, KD], F32, kind="ExternalInput")
        self.bom = self.dram("bom", [128, KD], F32, kind="ExternalInput")
        self.bvr = self.dram("bvr", [1, D], F32, kind="ExternalInput")
        self.ones128_d = self.dram("ones128", [128, 1], F32, kind="ExternalInput")
        self.ones1_d = self.dram("ones1", [1, 128], F32, kind="ExternalInput")
        self.ident_d = self.dram("ident", [128, 128], F32, kind="ExternalInput")
        self.w12c = self.dram("w12c", [E, 128, NI * 2 * D], BF16)
        self.w3c = self.dram("w3c", [E, 128, KD * I], BF16)
        self.w12sc = self.dram("w12sc", [128, NIS * 2 * D], BF16)
        self.w3sc = self.dram("w3sc", [128, KD * IS], BF16)
        self.xhat1 = self.dram("xhat1", [D, S], F32)
        self.n1 = self.dram("n1", [D, S], BF16)
        self.xhat2 = self.dram("xhat2", [D, S], F32)
        self.n2 = self.dram("n2", [D, S], BF16)
        self.att_s = self.dram("att_s", [D, S], BF16)
        self.cmb_scr = self.dram("cmb_scr", [64, 128], F32)

        sb = lambda n_, s, d=F32: self.ctx.enter_context(nc.sbuf_tensor(n_, s, d))
        self.slab = sb("slab", [128, self.SLAB_BYTES], mybir.dt.uint8)
        self.ones128_bf = sb("ones128_bf", [128, 1], BF16)
        self.ones128_f = sb("ones128_f", [128, 1], F32)
        self.ones1_f = sb("ones1_f", [1, 128], F32)
        self.ident = sb("identsb", [128, 128], F32)
        self.gwt_bf = sb("gwt_bf", [128, KD * E], BF16)
        self.bq_sb = sb("bq_sb", [128, KD], F32)
        self.bk_sb = sb("bk_sb", [128, KD], F32)
        self.bo_sb = sb("bo_sb", [128, KD], F32)
        self.pw = [self.ctx.enter_context(nc.psum_tensor(f"pw{i}", [128, 1024], F32))
                   for i in range(4)]

        consts = [
            ("gpsimd", self.ones128_bf[:, :], self.ones128_d, "c_ones128bf"),
            ("sync", self.ones128_f[:, :], self.ones128_d, "c_ones128f"),
            ("sync", self.ones1_f[:, :], self.ones1_d, "c_ones1"),
            ("sync", self.ident[:, :], self.ident_d, "c_ident"),
            ("gpsimd", self.gwt_bf[:, :], self.gwt, "c_gwt"),
            ("sync", self.bq_sb[:, :], self.bqm, "c_bq"),
            ("sync", self.bk_sb[:, :], self.bkm, "c_bk"),
            ("sync", self.bo_sb[:, :], self.bom, "c_bo"),
        ]
        for i, (eng, dst, src, key) in enumerate(consts):
            self.dma(eng, dst, src, f"d_c{i}", [], [key], f"ld_{key}")

        self.ffn_phase(phase=1)
        if cfg.get("do_p2", True):
            B.barrier()
            self.reset_slab()
            self.attn_phase_b()
            B.barrier()
            self.reset_slab()
            self.attn_phase_c()
        if cfg.get("do_p3", True):
            B.barrier()
            self.reset_slab()
            self.ffn_phase(phase=3)
        else:
            B.barrier()
            src, key = ((self.xhat2, "d_xhat2") if cfg.get("do_p2", True)
                        else (self.xhat1, "d_xhat1"))
            self.dma("sync", self.out_fm, src, "d_out", [key], ["d_out_fm"], "dbg_out")

        B.resolve()
        fin = [(k, v) for k, v in B.max_cums.items() if k.startswith("d_out") and v > 0]
        self.sems = B.playback(nc, self.ctx, fin)
        self.ctx.close()
        return nc

    def ffn_phase(self, phase):
        cfg = self.cfg
        self.reset_slab()
        c = self.carve
        self.stats = c("strow", [1, 2048], F32)
        bufs = dict(
            xbuf=c("xbuf", [128, KD * T], F32),
            xnbuf=c("xnbuf", [128, KD * T], BF16),
            hbuf=c("hbuf", [128, NI * T], BF16),
            w12sl=[c(f"w12_{i}", [128, 2 * D], BF16) for i in range(2)],
            w3sl=[c(f"w3_{i}", [128, 2 * I], BF16) for i in range(2)],
            tr=[c(f"tr{i}", [128, T], F32) for i in range(4)],
            sqb=[c(f"sq{i}", [128, T], BF16) for i in range(2)],
            combb=c("combb", [128, T], F32),
            meanb=c("meanb", [128, T], F32),
            rstdb=c("rstdb", [128, T], F32),
            lg=c("lg", [128, 64], F32),
            exs=c("exs", [128, 64], F32),
            pp=c("pp", [128, 64], F32),
            eq=c("eq", [128, 64], F32),
            keep=c("keep", [128, 64], F32),
            comb=c("comb", [128, 64], F32),
            mx=c("mx", [128, 8], F32),
            sm=c("sm", [128, 8], F32),
            combT=c("combT", [64, 128], F32),
            combe=[c(f"combe{i}", [1, T], F32) for i in range(2)],
        )
        first = (phase == 1)
        n_pass = cfg.get("passes", NPASS)
        for p in range(n_pass):
            pfx = f"p{phase}.{p}"
            tsl = slice(p * T, (p + 1) * T)
            use_cache = (not first) or (p > 0)
            if first:
                for j in range(4):
                    self.dma("sync",
                             bufs["xbuf"][:, j * 4 * T:(j + 1) * 4 * T].rearrange("q (k t) -> q k t", k=4),
                             self.x_fm.rearrange("(k q) t -> q k t", q=128)[:, 4 * j:4 * (j + 1), tsl],
                             f"d_xin{j}", ["d_x_fm"],
                             [f"xbuf.{4 * j + i}" for i in range(4)], f"{pfx}.ldx{j}")
                self.emit_ln(bufs, src_key="xbuf", pfx=f"{pfx}.ln1")
            else:
                for j in range(4):
                    self.dma("sync",
                             bufs["xnbuf"][:, j * 4 * T:(j + 1) * 4 * T].rearrange("q (k t) -> q k t", k=4),
                             self.n2.rearrange("(k q) t -> q k t", q=128)[:, 4 * j:4 * (j + 1), tsl],
                             f"d_xin{j}", ["d_n2"],
                             [f"xnbuf.{4 * j + i}" for i in range(4)], f"{pfx}.ldxn{j}")
            self.emit_gate(bufs, pfx=f"{pfx}.g")
            jobs = []
            if cfg.get("shared", True):
                jobs += [("s", 0), ("s", 1)]
            jobs += [("e", e) for e in range(cfg.get("experts", E))]
            for jn, (kind, e) in enumerate(jobs):
                self.emit_expert(kind, e, jn, bufs, use_cache,
                                 write_cache=(first and p == 0), pfx=pfx)
            self.emit_epilogue(phase, p, tsl, bufs)

    def emit_ln(self, bufs, src_key, pfx):
        B = self.B
        st = self.stats
        xf, xb, sqb, tr = bufs["xbuf"], bufs["xnbuf"], bufs["sqb"], bufs["tr"]
        meanb, rstdb = bufs["meanb"], bufs["rstdb"]
        pw = self.pw
        for k in range(KD):
            B.op("vector",
                 lambda v, k=k, xb=xb, xf=xf: v.tensor_copy(
                     xb[:, k * T:(k + 1) * T], xf[:, k * T:(k + 1) * T]),
                 reads=[f"{src_key}.{k}"], writes=[f"xnbuf.{k}"], tag=f"{pfx}.cp{k}")
            s = sqb[k % 2]
            B.op("scalar",
                 lambda sc, k=k, s=s, xb=xb: sc.activation(
                     s[:, :], xb[:, k * T:(k + 1) * T], AF.Square),
                 reads=[f"xnbuf.{k}"], writes=[f"sq{k % 2}"], tag=f"{pfx}.sq{k}")
            plan0 = [(pw[0][0:1, h * 512:(h + 1) * 512], self.ones128_bf[:, :],
                      xb[:, k * T + h * 512:k * T + (h + 1) * 512], k == 0, k == KD - 1)
                     for h in range(2)]
            self.mm_group(plan0, reads=[f"xnbuf.{k}", "c_ones128bf"],
                          writes=["pw0a", "pw0b"], tag=f"{pfx}.msm{k}")
            plan1 = [(pw[1][0:1, h * 512:(h + 1) * 512], self.ones128_bf[:, :],
                      s[:, h * 512:(h + 1) * 512], k == 0, k == KD - 1)
                     for h in range(2)]
            self.mm_group(plan1, reads=[f"sq{k % 2}", "c_ones128bf"],
                          writes=["pw1a", "pw1b"], tag=f"{pfx}.msq{k}")
        mean = st[0:1, 0:T]
        msq = st[0:1, T:2 * T]
        B.op("vector", lambda v, mean=mean: v.tensor_scalar_mul(mean[:, :], self.pw[0][0:1, 0:T], 1.0 / D),
             reads=["pw0a", "pw0b"], writes=["r_mean"], tag=f"{pfx}.mean")
        B.op("vector", lambda v, msq=msq: v.tensor_scalar_mul(msq[:, :], self.pw[1][0:1, 0:T], 1.0 / D),
             reads=["pw1a", "pw1b"], writes=["r_msq"], tag=f"{pfx}.msq")
        B.op("vector", lambda v, mean=mean: v.tensor_mul(self.pw[2][0:1, 0:T], mean[:, :], mean[:, :]),
             reads=["r_mean"], writes=["pw2a", "pw2b"], tag=f"{pfx}.m2")
        B.op("vector", lambda v, msq=msq: v.tensor_sub(msq[:, :], msq[:, :], self.pw[2][0:1, 0:T]),
             reads=["r_msq", "pw2a", "pw2b"], writes=["r_msq"], tag=f"{pfx}.var")
        B.op("vector", lambda v, msq=msq: v.tensor_scalar_add(msq[:, :], msq[:, :], EPS),
             reads=["r_msq"], writes=["r_msq"], tag=f"{pfx}.addeps")
        B.op("scalar", lambda s, msq=msq: s.activation(msq[:, :], msq[:, :], AF.Sqrt),
             reads=["r_msq"], writes=["r_msq"], tag=f"{pfx}.sqrt")
        B.op("vector", lambda v, msq=msq: v.reciprocal(msq[:, :], msq[:, :]),
             reads=["r_msq"], writes=["r_msq"], tag=f"{pfx}.recip")
        for h in range(2):
            self.mm_group([(pw[2][:, h * 512:(h + 1) * 512], self.ones1_f[:, :],
                            mean[:, h * 512:(h + 1) * 512], True, True)],
                          reads=["r_mean", "c_ones1"], writes=[f"pw2{'ab'[h]}"], tag=f"{pfx}.bm{h}")
            self.mm_group([(pw[3][:, h * 512:(h + 1) * 512], self.ones1_f[:, :],
                            msq[:, h * 512:(h + 1) * 512], True, True)],
                          reads=["r_msq", "c_ones1"], writes=[f"pw3{'ab'[h]}"], tag=f"{pfx}.br{h}")
        B.op("vector", lambda v, meanb=meanb: v.tensor_copy(meanb[:, :], self.pw[2][:, 0:T]),
             reads=["pw2a", "pw2b"], writes=["meanb"], tag=f"{pfx}.cbm")
        B.op("vector", lambda v, rstdb=rstdb: v.tensor_copy(rstdb[:, :], self.pw[3][:, 0:T]),
             reads=["pw3a", "pw3b"], writes=["rstdb"], tag=f"{pfx}.cbr")
        for k in range(KD):
            t = tr[k % 2]
            B.op("vector", lambda v, k=k, t=t, xf=xf, meanb=meanb: v.tensor_sub(
                 t[:, :], xf[:, k * T:(k + 1) * T], meanb[:, :]),
                 reads=[f"{src_key}.{k}", "meanb"], writes=[f"tr{k % 2}"], tag=f"{pfx}.sub{k}")
            B.op("vector", lambda v, k=k, t=t, xb=xb, rstdb=rstdb: v.tensor_mul(
                 xb[:, k * T:(k + 1) * T], t[:, :], rstdb[:, :]),
                 reads=[f"tr{k % 2}", "rstdb"], writes=[f"xnbuf.{k}"], tag=f"{pfx}.mul{k}")
    def emit_gate(self, bufs, pfx):
        B = self.B
        pw = self.pw
        NTC = T // 128
        xnbuf = bufs["xnbuf"]
        lg, exs, pp, eq, keep, comb = (bufs[k] for k in ("lg", "exs", "pp", "eq", "keep", "comb"))
        mx, sm, combT = bufs["mx"], bufs["sm"], bufs["combT"]
        plan = []
        for tc in range(NTC):
            for k in range(KD):
                plan.append((pw[0][:, tc * 8:(tc + 1) * 8],
                             xnbuf[:, k * T + tc * 128:k * T + (tc + 1) * 128],
                             self.gwt_bf[:, k * 8:(k + 1) * 8],
                             k == 0, k == KD - 1))
        self.mm_group(plan, reads=[f"xnbuf.{_k}" for _k in range(KD)] + ["c_gwt"], writes=["pw0a"], tag=f"{pfx}.lg")
        v3 = lambda ap: ap.rearrange("q (tc e) -> q tc e", e=E)
        r3 = lambda ap: ap.rearrange("q (tc o) -> q tc o", o=1)
        bc = lambda ap: r3(ap).broadcast_to((128, NTC, E))
        B.op("vector", lambda v: v.tensor_copy(lg[:, :], pw[0][:, 0:64]),
             reads=["pw0a"], writes=["lg"], tag=f"{pfx}.cp")
        B.op("vector", lambda v: v.reduce_max(r3(mx[:, :]), v3(lg[:, :]), axis=AX.X),
             reads=["lg"], writes=["mx"], tag=f"{pfx}.mx")
        B.op("vector", lambda v: v.tensor_tensor(v3(exs[:, :]), v3(lg[:, :]), bc(mx[:, :]), ALU.subtract),
             reads=["lg", "mx"], writes=["exs"], tag=f"{pfx}.subm")
        B.op("scalar", lambda s: s.activation(exs[:, :], exs[:, :], AF.Exp),
             reads=["exs"], writes=["exs"], tag=f"{pfx}.exp")
        B.op("vector", lambda v: v.reduce_sum(r3(sm[:, :]), v3(exs[:, :]), axis=AX.X),
             reads=["exs"], writes=["sm"], tag=f"{pfx}.sm")
        B.op("vector", lambda v: v.reciprocal(sm[:, :], sm[:, :]),
             reads=["sm"], writes=["sm"], tag=f"{pfx}.rc")
        B.op("vector", lambda v: v.tensor_tensor(v3(pp[:, :]), v3(exs[:, :]), bc(sm[:, :]), ALU.mult),
             reads=["exs", "sm"], writes=["pp"], tag=f"{pfx}.p")
        B.op("vector", lambda v: v.reduce_max(r3(mx[:, :]), v3(pp[:, :]), axis=AX.X),
             reads=["pp"], writes=["mx"], tag=f"{pfx}.m1")
        B.op("vector", lambda v: v.tensor_tensor(v3(eq[:, :]), v3(pp[:, :]), bc(mx[:, :]), ALU.is_ge),
             reads=["pp", "mx"], writes=["eq"], tag=f"{pfx}.eq")
        B.op("vector", lambda v: v.scalar_tensor_tensor(eq[:, :], eq[:, :], -2.0, pp[:, :], ALU.mult, ALU.add),
             reads=["eq", "pp"], writes=["eq"], tag=f"{pfx}.pm")
        B.op("vector", lambda v: v.reduce_max(r3(mx[:, :]), v3(eq[:, :]), axis=AX.X),
             reads=["eq"], writes=["mx"], tag=f"{pfx}.m2")
        B.op("vector", lambda v: v.tensor_tensor(v3(keep[:, :]), v3(pp[:, :]), bc(mx[:, :]), ALU.is_ge),
             reads=["pp", "mx"], writes=["keep"], tag=f"{pfx}.keep")
        B.op("vector", lambda v: v.tensor_mul(comb[:, :], pp[:, :], keep[:, :]),
             reads=["pp", "keep"], writes=["comb"], tag=f"{pfx}.comb")
        B.op("tensor", lambda t: t.transpose(pw[1][0:64, 0:128], comb[:, :], self.ident[:, :]),
             reads=["comb", "c_ident"], writes=["pw1a"], tag=f"{pfx}.tr")
        B.op("vector", lambda v: v.tensor_copy(combT[:, :], pw[1][0:64, 0:128]),
             reads=["pw1a"], writes=["combT"], tag=f"{pfx}.cpT")
        self.dma("scalar", self.cmb_scr, combT[:, :], "d_cmb", ["combT"], ["d_cmb_scr"], f"{pfx}.st")
    def emit_expert(self, kind, e, jn, bufs, use_cache, write_cache, pfx):
        B = self.B
        pw = self.pw
        pfx = f"{pfx}.x{kind}{e}"
        xnbuf, hbuf, xbuf, combb = (bufs[k] for k in ("xnbuf", "hbuf", "xbuf", "combb"))
        tr = bufs["tr"]
        if kind == "e":
            ce = bufs["combe"][e % 2]
            cekey = f"combe{e % 2}"
            self.dma("sync", ce[0:1, :].rearrange("q (tc j) -> q tc j", tc=8),
                     self.cmb_scr.rearrange("(tc e) j -> e tc j", tc=8, e=E)[e:e + 1, :, :],
                     f"d_ce{e % 2}", ["d_cmb_scr"], [cekey], f"{pfx}.ldce")
            for h in range(2):
                self.mm_group([(pw[2][:, h * 512:(h + 1) * 512], self.ones1_f[:, :],
                                ce[0:1, h * 512:(h + 1) * 512], True, True)],
                              reads=[cekey, "c_ones1"], writes=[f"pw2{'ab'[h]}"], tag=f"{pfx}.cb{h}")
            B.op("vector", lambda v, combb=combb: v.tensor_copy(combb[:, :], self.pw[2][:, 0:T]),
                 reads=["pw2a", "pw2b"], writes=["combb"], tag=f"{pfx}.cbc")

        for i in range(NI):
            sl = bufs["w12sl"][i % 2]
            slot = f"w12_{i % 2}"
            if kind == "e":
                src_f = self.w12[e, :, i * 2 * D:(i + 1) * 2 * D]
                src_c = self.w12c[e, :, i * 2 * D:(i + 1) * 2 * D]
            else:
                ii = e * NI + i
                src_f = self.w12s[:, ii * 2 * D:(ii + 1) * 2 * D]
                src_c = self.w12sc[:, ii * 2 * D:(ii + 1) * 2 * D]
            ckey = f"wc12.{kind}{e}.{i}"
            weng = "gpsimd" if i % 2 == 0 else "sync"
            if use_cache:
                self.dma(weng, sl[:, :], src_c, f"d_{slot}", [ckey], [slot], f"{pfx}.wl{i}")
            else:
                self.dma("gpsimd", sl[:, :], src_f, f"d_{slot}", [], [slot], f"{pfx}.wl{i}")
                if write_cache:
                    self.dma("scalar", src_c, sl[:, :], f"d_{slot}", [slot], [ckey], f"{pfx}.ws{i}")
            for which in range(2):
                plan = []
                for h in range(2):
                    for k in range(KD):
                        plan.append((pw[2 + which][:, h * 512:(h + 1) * 512],
                                     sl[:, which * D + k * 128:which * D + (k + 1) * 128],
                                     xnbuf[:, k * T + h * 512:k * T + (h + 1) * 512],
                                     k == 0, k == KD - 1))
                self.mm_group(plan, reads=[slot] + [f"xnbuf.{_k}" for _k in range(KD)],
                              writes=[f"pw{2 + which}a", f"pw{2 + which}b"],
                              tag=f"{pfx}.h{which}.{i}")
            sil = tr[i % 2]
            ht = tr[2 + i % 2]
            B.op("scalar", lambda s, sil=sil: s.activation(sil[:, :], self.pw[2][:, 0:T], AF.Silu),
                 reads=["pw2a", "pw2b"], writes=[f"tr{i % 2}"], tag=f"{pfx}.sa{i}")
            B.op("vector", lambda v, sil=sil, ht=ht: v.tensor_mul(ht[:, :], sil[:, :], self.pw[3][:, 0:T]),
                 reads=[f"tr{i % 2}", "pw3a", "pw3b"], writes=[f"tr{2 + i % 2}"], tag=f"{pfx}.hm{i}")
            if kind == "e":
                B.op("vector", lambda v, ht=ht, i=i, hbuf=hbuf, combb=combb: v.tensor_mul(
                     hbuf[:, i * T:(i + 1) * T], ht[:, :], combb[:, :]),
                     reads=[f"tr{2 + i % 2}", "combb"], writes=["hbuf"], tag=f"{pfx}.hc{i}")
            else:
                B.op("vector", lambda v, ht=ht, i=i, hbuf=hbuf: v.tensor_copy(
                     hbuf[:, i * T:(i + 1) * T], ht[:, :]),
                     reads=[f"tr{2 + i % 2}"], writes=["hbuf"], tag=f"{pfx}.hp{i}")

        for dp in range(KD // 2):
            sl = bufs["w3sl"][dp % 2]
            slot = f"w3_{dp % 2}"
            ckey = f"wc3.{kind}{e}.{dp}"
            weng3 = "gpsimd" if dp % 2 == 0 else "scalar"
            if kind == "e":
                src_f = self.w3b[e, :, dp * 2 * I:(dp + 1) * 2 * I]
                src_c = self.w3c[e, :, dp * 2 * I:(dp + 1) * 2 * I]
                if use_cache:
                    self.dma(weng3, sl[:, :], src_c, f"d_{slot}", [ckey], [slot], f"{pfx}.w3l{dp}")
                else:
                    self.dma("gpsimd", sl[:, :], src_f, f"d_{slot}", [], [slot], f"{pfx}.w3l{dp}")
                    if write_cache:
                        self.dma("scalar", src_c, sl[:, :], f"d_{slot}", [slot], [ckey], f"{pfx}.w3s{dp}")
            else:
                big = self.w3sc if use_cache else self.w3s
                srcv = big.rearrange("q (d ki j) -> q d ki j", d=KD, ki=NIS)[
                    :, 2 * dp:2 * dp + 2, e * NI:(e + 1) * NI, :]
                dstv = sl[:, :].rearrange("q (d ki j) -> q d ki j", d=2, ki=NI)
                if use_cache:
                    self.dma(weng3, dstv, srcv, f"d_{slot}", [ckey], [slot], f"{pfx}.w3l{dp}")
                else:
                    self.dma("gpsimd", dstv, srcv, f"d_{slot}", [], [slot], f"{pfx}.w3l{dp}")
                    if write_cache:
                        dstc = self.w3sc.rearrange("q (d ki j) -> q d ki j", d=KD, ki=NIS)[
                            :, 2 * dp:2 * dp + 2, e * NI:(e + 1) * NI, :]
                        self.dma("scalar", dstc, dstv, f"d_{slot}", [slot], [ckey], f"{pfx}.w3s{dp}")
            for dd in range(2):
                d = dp * 2 + dd
                plan = []
                for h in range(2):
                    for ki in range(NI):
                        plan.append((pw[dd][:, h * 512:(h + 1) * 512],
                                     sl[:, dd * I + ki * 128:dd * I + (ki + 1) * 128],
                                     hbuf[:, ki * T + h * 512:ki * T + (h + 1) * 512],
                                     ki == 0, ki == NI - 1))
                self.mm_group(plan, reads=[slot, "hbuf"],
                              writes=[f"pw{dd}a", f"pw{dd}b"], tag=f"{pfx}.w3d{d}")
                if jn == 0:
                    B.op("vector", lambda v, d=d, dd=dd, xbuf=xbuf: v.tensor_copy(
                         xbuf[:, d * T:(d + 1) * T], self.pw[dd][:, 0:T]),
                         reads=[f"pw{dd}a", f"pw{dd}b"], writes=[f"xbuf.{d}"], tag=f"{pfx}.ac{d}")
                else:
                    B.op("vector", lambda v, d=d, dd=dd, xbuf=xbuf: v.tensor_add(
                         xbuf[:, d * T:(d + 1) * T], xbuf[:, d * T:(d + 1) * T], self.pw[dd][:, 0:T]),
                         reads=[f"pw{dd}a", f"pw{dd}b", f"xbuf.{d}"], writes=[f"xbuf.{d}"], tag=f"{pfx}.aa{d}")
    def emit_epilogue(self, phase, p, tsl, bufs):
        B = self.B
        pfx = f"p{phase}.{p}.ep"
        first = (phase == 1)
        xbuf, xnbuf, tr = bufs["xbuf"], bufs["xnbuf"], bufs["tr"]
        src = self.x_fm if first else self.xhat2
        srckey = "d_x_fm" if first else "d_xhat2"
        for k in range(KD):
            xe = tr[k % 2]
            self.dma("sync", xe[:, :],
                     src.rearrange("(k q) t -> q k t", q=128)[:, k, tsl.start:tsl.stop],
                     f"d_xep{k % 2}", [srckey], [f"tr{k % 2}"], f"{pfx}.ldx{k}")
            B.op("vector",
                 lambda v, k=k, xe=xe, xbuf=xbuf: v.scalar_tensor_tensor(
                     xbuf[:, k * T:(k + 1) * T], xbuf[:, k * T:(k + 1) * T], 0.5,
                     xe[:, :], ALU.mult, ALU.add),
                 reads=[f"xbuf.{k}", f"tr{k % 2}"], writes=[f"xbuf.{k}"], tag=f"{pfx}.x1{k}")
        if first:
            for j in range(4):
                self.dma("scalar",
                         self.xhat1.rearrange("(k q) t -> q k t", q=128)[:, 4 * j:4 * (j + 1), tsl],
                         xbuf[:, j * 4 * T:(j + 1) * 4 * T].rearrange("q (k t) -> q k t", k=4),
                         f"d_x1st{j}", [f"xbuf.{4 * j + i}" for i in range(4)],
                         ["d_xhat1"], f"{pfx}.stx1{j}")
            self.emit_ln(bufs, src_key="xbuf", pfx=f"{pfx}.ln2")
            for j in range(4):
                self.dma("scalar",
                         self.n1.rearrange("(k q) t -> q k t", q=128)[:, 4 * j:4 * (j + 1), tsl],
                         xnbuf[:, j * 4 * T:(j + 1) * 4 * T].rearrange("q (k t) -> q k t", k=4),
                         f"d_n1st{j}", [f"xnbuf.{4 * j + i}" for i in range(4)],
                         ["d_n1"], f"{pfx}.stn1{j}")
        else:
            for j in range(4):
                self.dma("scalar",
                         self.out_fm.rearrange("(k q) t -> q k t", q=128)[:, 4 * j:4 * (j + 1), tsl],
                         xbuf[:, j * 4 * T:(j + 1) * 4 * T].rearrange("q (k t) -> q k t", k=4),
                         f"d_out{j}", [f"xbuf.{4 * j + i}" for i in range(4)],
                         ["d_out_fm"], f"{pfx}.stout{j}")

    def attn_phase_b(self):
        B = self.B
        c = self.carve
        pw = self.pw
        self.stats = c("strow2", [1, 1152], F32)
        st = self.stats
        self.bvr_sb = c("bvr_sb", [1, D], F32)
        self.dma("sync", self.bvr_sb[:, :], self.bvr, "d_cbv", [], ["c_bv"], "ld_c_bv")
        xn1 = c("xn1", [128, KD * S], BF16)
        qb = c("q_bf", [128, S], BF16)
        kb = c("k_bf", [128, S], BF16)
        vb = c("v_bf", [128, S], BF16)
        ab = c("att_h", [128, S], BF16)
        wsl = [c(f"aw{i}", [128, KD * 128], BF16) for i in range(3)]
        exb = [c(f"expT{i}", [128, 1024], BF16) for i in range(2)]
        rcb = [c(f"rcb{i}", [128, 512], F32) for i in range(2)]
        sqt = c("sqt", [128, 1024], BF16)
        bias_c = c("bias_c", [128, 4], F32)
        bvb = c("bvb", [128, 128], F32)
        for j in range(4):
            self.dma("sync",
                     xn1[:, :].rearrange("q (k t) -> q k t", k=KD)[:, :, j * 1024:(j + 1) * 1024],
                     self.n1.rearrange("(k q) t -> q k t", q=128)[:, :, j * 1024:(j + 1) * 1024],
                     f"d_xn1{j}", ["d_n1"], [f"xn1.{j}"], f"p2.ldxn1{j}")

        def load_w(h, pfx):
            for (i, (dsrc, nm)) in enumerate(((self.wqb, "q"), (self.wkb, "k"), (self.wvb, "v"))):
                self.dma("gpsimd", wsl[i][:, :], dsrc[:, h * D:(h + 1) * D],
                         f"d_aw{nm}", [], [f"aw{nm}"], f"{pfx}.w{nm}")

        n_heads = self.cfg.get("heads", H)
        load_w(0, "p2.h0pre")
        for h in range(n_heads):
            pfx = f"p2.h{h}"
            wq, wk, wv = wsl
            self.mm_group([(pw[3][:, 0:128], self.ones1_f[:, :],
                            self.bvr_sb[0:1, h * 128:(h + 1) * 128], True, True)],
                          reads=["c_bv", "c_ones1"], writes=["pw3a"], tag=f"{pfx}.bvb")
            B.op("vector", lambda v, bvb=bvb: v.tensor_copy(bvb[:, :], self.pw[3][:, 0:128]),
                 reads=["pw3a"], writes=["bvb"], tag=f"{pfx}.bvc")
            for (w, bmat, dst, dkey, nm) in ((wq, self.bq_sb, qb, "qbf", "q"),
                                             (wk, self.bk_sb, kb, "kbf", "k")):
                wkey = f"aw{nm}"
                for tq in range(8):
                    hp = tq % 2
                    plan = [(pw[2][:, hp * 512:(hp + 1) * 512], w[:, k * 128:(k + 1) * 128],
                             xn1[:, k * S + tq * 512:k * S + (tq + 1) * 512],
                             k == 0, k == KD - 1) for k in range(KD)]
                    self.mm_group(plan, reads=[wkey, f"xn1.{tq // 2}"], writes=[f"pw2{'ab'[hp]}"], tag=f"{pfx}.{nm}m{tq}")
                    B.op("scalar", lambda s, tq=tq, hp=hp, dst=dst, bmat=bmat, h=h: s.add(
                         dst[:, tq * 512:(tq + 1) * 512], self.pw[2][:, hp * 512:(hp + 1) * 512],
                         bmat[:, h:h + 1]),
                         reads=[f"pw2{'ab'[hp]}", "c_b" + nm], writes=[dkey], tag=f"{pfx}.{nm}e{tq}")
            for tc in range(32):
                hp = tc % 2
                plan = [(pw[3][:, hp * 512:hp * 512 + 128],
                         xn1[:, k * S + tc * 128:k * S + (tc + 1) * 128],
                         wv[:, k * 128:(k + 1) * 128],
                         k == 0, k == KD - 1) for k in range(KD)]
                self.mm_group(plan, reads=["awv", f"xn1.{tc // 8}"], writes=[f"pw3{'ab'[hp]}"], tag=f"{pfx}.vm{tc}")
                B.op("vector", lambda v, tc=tc, hp=hp, vb=vb, bvb=bvb: v.tensor_tensor(
                     vb[:, tc * 128:(tc + 1) * 128], self.pw[3][:, hp * 512:hp * 512 + 128],
                     bvb[:, :], ALU.add),
                     reads=[f"pw3{'ab'[hp]}", "bvb"], writes=["vbf"], tag=f"{pfx}.ve{tc}")
            if h + 1 < n_heads:
                load_w(h + 1, f"p2.h{h + 1}pre")
            for o, (buf, bkey) in enumerate(((qb, "qbf"), (kb, "kbf"))):
                for tq in range(8):
                    i = o * 8 + tq
                    par = i % 2
                    sq_half = sqt[:, par * 512:(par + 1) * 512]
                    if o == 0:
                        B.op("scalar", lambda s, buf=buf, tq=tq, sq_half=sq_half: s.activation(
                             sq_half, buf[:, tq * 512:(tq + 1) * 512], AF.Square),
                             reads=[bkey], writes=[f"sqt{par}"], tag=f"{pfx}.n{o}.{tq}")
                    else:
                        B.op("vector", lambda v, buf=buf, tq=tq, sq_half=sq_half: v.tensor_mul(
                             sq_half, buf[:, tq * 512:(tq + 1) * 512], buf[:, tq * 512:(tq + 1) * 512]),
                             reads=[bkey], writes=[f"sqt{par}"], tag=f"{pfx}.n{o}.{tq}")
                    self.mm_group([(pw[2][0:1, par * 512:par * 512 + 512], self.ones128_bf[:, :],
                                    sq_half, True, True)],
                                  reads=[f"sqt{par}", "c_ones128bf"], writes=[f"pw2{'ab'[par]}"],
                                  tag=f"{pfx}.nm{o}.{tq}")
                    B.op("vector", lambda v, o=o, tq=tq, par=par: v.reduce_max(
                         st[0:1, o * 8 + tq:o * 8 + tq + 1],
                         self.pw[2][0:1, par * 512:par * 512 + 512], axis=AX.X),
                         reads=[f"pw2{'ab'[par]}"], writes=["r_nrm"], tag=f"{pfx}.nx{o}.{tq}")
            B.op("vector", lambda v: v.reduce_max(st[0:1, 16:17], st[0:1, 0:8], axis=AX.X),
                 reads=["r_nrm"], writes=["r_nrm"], tag=f"{pfx}.qm")
            B.op("vector", lambda v: v.reduce_max(st[0:1, 17:18], st[0:1, 8:16], axis=AX.X),
                 reads=["r_nrm"], writes=["r_nrm"], tag=f"{pfx}.km")
            B.op("vector", lambda v: v.tensor_add(st[0:1, 18:19], st[0:1, 16:17], st[0:1, 17:18]),
                 reads=["r_nrm"], writes=["r_nrm"], tag=f"{pfx}.sm2")
            B.op("vector", lambda v: v.tensor_scalar_mul(st[0:1, 19:20], st[0:1, 18:19], -0.5 * SCALE),
                 reads=["r_nrm"], writes=["r_nrm"], tag=f"{pfx}.ng")
            self.mm_group([(pw[2][:, 0:4], self.ones1_f[:, :], st[0:1, 16:20], True, True)],
                          reads=["r_nrm", "c_ones1"], writes=["pw2a"], tag=f"{pfx}.cb")
            B.op("vector", lambda v, bias_c=bias_c: v.tensor_copy(bias_c[:, :], self.pw[2][:, 0:4]),
                 reads=["pw2a"], writes=["biasc"], tag=f"{pfx}.cbc")
            pending_norm = None
            for qc in range(8):
                qp = qc % 2
                for it in range(17):
                    if it < 16:
                        pp = it % 2
                        plan_s = [(pw[pp][:, hh * 512:(hh + 1) * 512],
                                   kb[:, (2 * it + hh) * 128:(2 * it + hh + 1) * 128],
                                   qb[:, qc * 512:(qc + 1) * 512], True, True)
                                  for hh in range(2)]
                        self.mm_group(plan_s, reads=["kbf", "qbf"],
                                      writes=[f"pw{pp}a", f"pw{pp}b"], tag=f"{pfx}.s{qc}.{it}")
                    if it == 0 and pending_norm is not None:
                        pending_norm()
                        pending_norm = None
                    if it >= 1:
                        pr = it - 1
                        pp = pr % 2
                        B.op("scalar", lambda s, pp=pp, exb=exb, bias_c=bias_c: s.activation(
                             exb[pp][:, :], self.pw[pp][:, 0:1024], AF.Exp,
                             bias=bias_c[:, 3:4], scale=SCALE),
                             reads=[f"pw{pp}a", f"pw{pp}b", "biasc"], writes=[f"expT{pp}"],
                             tag=f"{pfx}.e{qc}.{pr}")
                        plan_pv = []
                        plan_dn = []
                        for s2 in range(2):
                            kk2 = 2 * pr + s2
                            plan_pv.append((pw[2][:, qp * 512:(qp + 1) * 512],
                                            vb[:, kk2 * 128:(kk2 + 1) * 128],
                                            exb[pp][:, s2 * 512:(s2 + 1) * 512],
                                            kk2 == 0, kk2 == 31))
                            plan_dn.append((pw[3][0:1, qp * 512:(qp + 1) * 512],
                                            self.ones128_bf[:, :],
                                            exb[pp][:, s2 * 512:(s2 + 1) * 512],
                                            kk2 == 0, kk2 == 31))
                        self.mm_group(plan_dn, reads=[f"expT{pp}", "c_ones128bf"],
                                      writes=[f"pw3{'ab'[qp]}"], tag=f"{pfx}.d{qc}.{pr}")
                        self.mm_group(plan_pv, reads=[f"expT{pp}", "vbf"],
                                      writes=[f"pw2{'ab'[qp]}"], tag=f"{pfx}.o{qc}.{pr}")
                def emit_norm(qc=qc, qp=qp):
                    B.op("vector", lambda v, qp=qp: v.reciprocal(
                         st[0:1, 128 + qp * 512:128 + (qp + 1) * 512], self.pw[3][0:1, qp * 512:(qp + 1) * 512]),
                         reads=[f"pw3{'ab'[qp]}"], writes=[f"r_den{qp}"], tag=f"{pfx}.rc{qc}")
                    self.mm_group([(pw[3][:, qp * 512:(qp + 1) * 512], self.ones1_f[:, :],
                                    st[0:1, 128 + qp * 512:128 + (qp + 1) * 512], True, True)],
                                  reads=[f"r_den{qp}", "c_ones1"], writes=[f"pw3{'ab'[qp]}"], tag=f"{pfx}.rb{qc}")
                    B.op("vector", lambda v, qp=qp, rcb=rcb: v.tensor_copy(
                         rcb[qp][:, :], self.pw[3][:, qp * 512:(qp + 1) * 512]),
                         reads=[f"pw3{'ab'[qp]}"], writes=[f"rcbx{qp}"], tag=f"{pfx}.rv{qc}")
                    B.op("vector", lambda v, qc=qc, qp=qp, ab=ab, rcb=rcb: v.tensor_tensor(
                         ab[:, qc * 512:(qc + 1) * 512], self.pw[2][:, qp * 512:(qp + 1) * 512],
                         rcb[qp][:, :], ALU.mult),
                         reads=[f"pw2{'ab'[qp]}", f"rcbx{qp}"], writes=["att"], tag=f"{pfx}.at{qc}")
                pending_norm = emit_norm
            pending_norm()
            pending_norm = None
            self.dma("scalar", self.att_s[h * 128:(h + 1) * 128, :], ab[:, :],
                     "d_attst", ["att"], ["d_att_s"], f"{pfx}.st")
    def attn_phase_c(self):
        B = self.B
        c = self.carve
        pw = self.pw
        self.stats = c("strow3", [1, 2048], F32)
        st = self.stats
        wo = c("wo_bf", [128, KD * D], BF16)
        att = [c(f"attt{i}", [128, KD * 512], BF16) for i in range(2)]
        x1t = c("x1t", [128, KD * 512], F32)
        x2t = c("x2t", [128, KD * 512], F32)
        xn2t = c("xn2t", [128, KD * 512], BF16)
        sqt = c("sqt2", [128, 1024], BF16)
        xbt = c("xbt", [128, 1024], BF16)
        tln = c("tln", [128, 512], F32)
        meanb = c("meanb2", [128, 512], F32)
        rstdb = c("rstdb2", [128, 512], F32)
        for j in range(4):
            self.dma("gpsimd", wo[:, j * 4 * D:(j + 1) * 4 * D],
                     self.wob[:, j * 4 * D:(j + 1) * 4 * D],
                     f"d_wo{j}", [], [f"wo.{j}"], f"p2c.ldwo{j}")
        for ts in range(8):
            pfx = f"p2c.t{ts}"
            par = ts % 2
            self.dma("sync", att[par][:, :].rearrange("q (k t) -> q k t", k=KD),
                     self.att_s.rearrange("(k q) t -> q k t", q=128)[:, :, ts * 512:(ts + 1) * 512],
                     f"d_att{par}", ["d_att_s"], [f"attt{par}"], f"{pfx}.la")
            for qh, eng in ((0, "sync"), (1, "scalar")):
                self.dma(eng, x1t[:, qh * 8 * 512:(qh + 1) * 8 * 512].rearrange("q (k t) -> q k t", k=KD // 2),
                         self.xhat1.rearrange("(k q) t -> q k t", q=128)[
                             :, qh * 8:(qh + 1) * 8, ts * 512:(ts + 1) * 512],
                         f"d_x1t{qh}", ["d_xhat1"], ["x1t"], f"{pfx}.lx{qh}")
            stat_pending = None
            for d in range(KD):
                wt_ = pw[(d % 4) // 2]
                hp = d % 2
                pkey = f"pw{(d % 4) // 2}{'ab'[hp]}"
                plan = [(wt_[:, hp * 512:(hp + 1) * 512], wo[:, d * D + k * 128:d * D + (k + 1) * 128],
                         att[par][:, k * 512:(k + 1) * 512], k == 0, k == KD - 1)
                        for k in range(KD)]
                self.mm_group(plan, reads=[f"wo.{d // 4}", f"attt{par}"], writes=[pkey], tag=f"{pfx}.m{d}")
                B.op("vector", lambda v, d=d, wt_=wt_, hp=hp, x2t=x2t, x1t=x1t: v.scalar_tensor_tensor(
                     x2t[:, d * 512:(d + 1) * 512], wt_[:, hp * 512:(hp + 1) * 512], self.bo_sb[:, d:d + 1],
                     x1t[:, d * 512:(d + 1) * 512], ALU.add, ALU.add),
                     reads=[pkey, "x1t", "c_bo"], writes=["x2t"], tag=f"{pfx}.r{d}")
                dp2 = d % 2
                xb_h = xbt[:, dp2 * 512:(dp2 + 1) * 512]
                sq_h = sqt[:, dp2 * 512:(dp2 + 1) * 512]
                B.op("vector", lambda v, d=d, x2t=x2t, xb_h=xb_h: v.tensor_copy(
                     xb_h, x2t[:, d * 512:(d + 1) * 512]),
                     reads=["x2t"], writes=[f"xbt{dp2}"], tag=f"{pfx}.c{d}")
                B.op("scalar", lambda s, sq_h=sq_h, xb_h=xb_h: s.activation(sq_h, xb_h, AF.Square),
                     reads=[f"xbt{dp2}"], writes=[f"sqt2{dp2}"], tag=f"{pfx}.q{d}")
                if stat_pending is not None:
                    stat_pending()

                def emit_stats(d=d, dp2=dp2, xb_h=xb_h, sq_h=sq_h):
                    self.mm_group([(pw[2][0:1, 0:512], self.ones128_bf[:, :], xb_h,
                                    d == 0, d == KD - 1)],
                                  reads=[f"xbt{dp2}", "c_ones128bf"], writes=["pw2a"], tag=f"{pfx}.s{d}")
                    self.mm_group([(pw[2][0:1, 512:1024], self.ones128_bf[:, :], sq_h,
                                    d == 0, d == KD - 1)],
                                  reads=[f"sqt2{dp2}", "c_ones128bf"], writes=["pw2b"], tag=f"{pfx}.z{d}")
                stat_pending = emit_stats
            stat_pending()
            stat_pending = None
            B.op("vector", lambda v: v.tensor_scalar_mul(st[0:1, 0:512], self.pw[2][0:1, 0:512], 1.0 / D),
                 reads=["pw2a"], writes=["r_mean"], tag=f"{pfx}.mn")
            B.op("vector", lambda v: v.tensor_scalar_mul(st[0:1, 512:1024], self.pw[2][0:1, 512:1024], 1.0 / D),
                 reads=["pw2b"], writes=["r_msq"], tag=f"{pfx}.mq")
            B.op("vector", lambda v: v.tensor_mul(self.pw[3][0:1, 0:512], st[0:1, 0:512], st[0:1, 0:512]),
                 reads=["r_mean"], writes=["pw3a"], tag=f"{pfx}.m2")
            B.op("vector", lambda v: v.tensor_sub(st[0:1, 512:1024], st[0:1, 512:1024], self.pw[3][0:1, 0:512]),
                 reads=["r_msq", "pw3a"], writes=["r_msq"], tag=f"{pfx}.va")
            B.op("vector", lambda v: v.tensor_scalar_add(st[0:1, 512:1024], st[0:1, 512:1024], EPS),
                 reads=["r_msq"], writes=["r_msq"], tag=f"{pfx}.ae")
            B.op("scalar", lambda s: s.activation(st[0:1, 512:1024], st[0:1, 512:1024], AF.Sqrt),
                 reads=["r_msq"], writes=["r_msq"], tag=f"{pfx}.sq")
            B.op("vector", lambda v: v.reciprocal(st[0:1, 512:1024], st[0:1, 512:1024]),
                 reads=["r_msq"], writes=["r_msq"], tag=f"{pfx}.rc")
            self.mm_group([(pw[3][:, 0:512], self.ones1_f[:, :], st[0:1, 0:512], True, True)],
                          reads=["r_mean", "c_ones1"], writes=["pw3a"], tag=f"{pfx}.bm")
            self.mm_group([(pw[3][:, 512:1024], self.ones1_f[:, :], st[0:1, 512:1024], True, True)],
                          reads=["r_msq", "c_ones1"], writes=["pw3b"], tag=f"{pfx}.br")
            B.op("vector", lambda v, meanb=meanb: v.tensor_copy(meanb[:, :], self.pw[3][:, 0:512]),
                 reads=["pw3a"], writes=["meanb2"], tag=f"{pfx}.bc")
            B.op("vector", lambda v, rstdb=rstdb: v.tensor_copy(rstdb[:, :], self.pw[3][:, 512:1024]),
                 reads=["pw3b"], writes=["rstdb2"], tag=f"{pfx}.rb")
            for d in range(KD):
                B.op("vector", lambda v, d=d, tln=tln, x2t=x2t, meanb=meanb: v.tensor_sub(
                     tln[:, :], x2t[:, d * 512:(d + 1) * 512], meanb[:, :]),
                     reads=["x2t", "meanb2"], writes=["tln"], tag=f"{pfx}.u{d}")
                B.op("vector", lambda v, d=d, tln=tln, xn2t=xn2t, rstdb=rstdb: v.tensor_mul(
                     xn2t[:, d * 512:(d + 1) * 512], tln[:, :], rstdb[:, :]),
                     reads=["tln", "rstdb2"], writes=["xn2t"], tag=f"{pfx}.w{d}")
            self.dma("gpsimd",
                     self.xhat2.rearrange("(k q) t -> q k t", q=128)[:, :, ts * 512:(ts + 1) * 512],
                     x2t[:, :].rearrange("q (k t) -> q k t", k=KD),
                     "d_x2st", ["x2t"], ["d_xhat2"], f"{pfx}.sx")
            self.dma("scalar",
                     self.n2.rearrange("(k q) t -> q k t", q=128)[:, :, ts * 512:(ts + 1) * 512],
                     xn2t[:, :].rearrange("q (k t) -> q k t", k=KD),
                     "d_n2st", ["xn2t"], ["d_n2"], f"{pfx}.sn")
def build_program(cfg=None):
    mk = MK(cfg or {})
    nc = mk.build()
    return nc


def kernel(**inputs):
    per_core = prep_inputs(inputs)
    nc = build_program({})
    res = run_bass_kernel_spmd(nc, per_core, core_ids=list(range(8)))
    out = np.stack([np.ascontiguousarray(np.asarray(r["out_fm"]).T) for r in res.results])
    return out.astype(np.float32)
